# version 05
# baseline (speedup 1.0000x reference)
import math
import numpy as np
import ml_dtypes
from contextlib import ExitStack

import concourse.bass as bass
import concourse.mybir as mybir
from concourse.bass_utils import run_bass_kernel_spmd

F32 = mybir.dt.float32
BF16 = mybir.dt.bfloat16
I32 = mybir.dt.int32
AF = mybir.ActivationFunctionType
ALU = mybir.AluOpType
BF = ml_dtypes.bfloat16

NCORES = 8
B = 2
S = 4096
D = 4096
NH = 32
NKV = 8
HD = 128
TOK = 1024
I_FFN = 11008
NG_GU = 32
NG_D = 11


class Sem:
    def __init__(self, handle):
        self.h = handle
        self.n = 0

    def inc(self, instr, k=1):
        instr.then_inc(self.h, k)
        self.n += k
        return self.n



def build_launch_A():
    nc = bass.Bass("TRN2", target_bir_lowering=False, debug=False)
    xT = nc.dram_tensor("xT", [D, TOK], F32, kind="ExternalInput").ap()
    wq = nc.dram_tensor("wq", [D, D], F32, kind="ExternalInput").ap()
    wk = nc.dram_tensor("wk", [D, NKV * HD], F32, kind="ExternalInput").ap()
    wv = nc.dram_tensor("wv", [D, NKV * HD], F32, kind="ExternalInput").ap()
    cos0 = nc.dram_tensor("cos0", [TOK, 64], F32, kind="ExternalInput").ap()
    sin0 = nc.dram_tensor("sin0", [TOK, 64], F32, kind="ExternalInput").ap()
    gq = nc.dram_tensor("gq", [D, 172], I32, kind="ExternalInput").ap()
    gz = nc.dram_tensor("gz", [NG_GU, 172], I32, kind="ExternalInput").ap()
    gs = nc.dram_tensor("gs", [NG_GU, 1376], F32, kind="ExternalInput").ap()
    uq = nc.dram_tensor("uq", [D, 172], I32, kind="ExternalInput").ap()
    uz = nc.dram_tensor("uz", [NG_GU, 172], I32, kind="ExternalInput").ap()
    us = nc.dram_tensor("us", [NG_GU, 1376], F32, kind="ExternalInput").ap()
    dqw = nc.dram_tensor("dqw", [NG_D * 128, 512], I32, kind="ExternalInput").ap()
    dz = nc.dram_tensor("dz", [NG_D, 512], I32, kind="ExternalInput").ap()
    ds = nc.dram_tensor("ds", [NG_D, D], F32, kind="ExternalInput").ap()

    qo = nc.dram_tensor("qo", [TOK, D], BF16, kind="ExternalOutput").ap()
    ko = nc.dram_tensor("ko", [TOK, NKV * HD], BF16, kind="ExternalOutput").ap()
    vo = nc.dram_tensor("vo", [TOK, NKV * HD], BF16, kind="ExternalOutput").ap()
    wgb = nc.dram_tensor("wgb", [D, 1376], BF16, kind="ExternalOutput").ap()
    wub = nc.dram_tensor("wub", [D, 1376], BF16, kind="ExternalOutput").ap()
    wdb = nc.dram_tensor("wdb", [NG_D * 128, D], BF16, kind="ExternalOutput").ap()
    zscr = nc.dram_tensor("zscr", [2 * NG_GU + NG_D, 4096], F32, kind="Internal").ap()

    wsrc = [(wq, dc * 512, True) for dc in range(8)] + \
           [(wk, dc * 512, True) for dc in range(2)] + \
           [(wv, dc * 512, False) for dc in range(2)]
    outmap = [(qo, dc * 512) for dc in range(8)] + \
             [(ko, dc * 512) for dc in range(2)] + \
             [(vo, dc * 512) for dc in range(2)]

    dplans = [
        (gq, gs, wgb, NG_GU, 172, 1376, 0),
        (uq, us, wub, NG_GU, 172, 1376, NG_GU),
        (dqw, ds, wdb, NG_D, 512, 4096, 2 * NG_GU),
    ]

    with ExitStack() as ctx:
        e = ctx.enter_context
        xb = e(nc.sbuf_tensor("xb", [128, 32 * TOK], BF16))
        wbuf = [e(nc.sbuf_tensor(f"wbuf{i}", [128, 32 * 512], BF16)) for i in range(2)]
        cosb = e(nc.sbuf_tensor("cosb", [128, 8 * 64], F32))
        sinb = e(nc.sbuf_tensor("sinb", [128, 8 * 64], F32))
        t1 = e(nc.sbuf_tensor("t1", [128, 256], F32))
        t2 = e(nc.sbuf_tensor("t2", [128, 256], F32))
        pout = [e(nc.sbuf_tensor(f"pout{i}", [128, 512], BF16)) for i in range(2)]
        ps = [e(nc.psum_tensor(f"ps{i}", [128, 512], F32)) for i in range(2)]
        zrow = [e(nc.sbuf_tensor(f"zrow{p}", [NG_GU, 512], I32)) for p in range(3)]
        zf = e(nc.sbuf_tensor("zf", [NG_GU, 4096], I32))
        zf32 = e(nc.sbuf_tensor("zf32", [NG_GU, 4096], F32))
        qslab = e(nc.sbuf_tensor("qslab", [128, 512], I32))
        nibi = e(nc.sbuf_tensor("nibi", [128, 2048], I32))
        nibf = e(nc.sbuf_tensor("nibf", [128, 2048], BF16))
        wslab = e(nc.sbuf_tensor("wslab", [128, 2048], BF16))
        zbc = e(nc.sbuf_tensor("zbc", [128, 2048], BF16))
        sbc = e(nc.sbuf_tensor("sbc", [128, 2048], BF16))

        ld = Sem(e(nc.semaphore("ld")))
        ldg = Sem(e(nc.semaphore("ldg")))
        lw = [Sem(e(nc.semaphore("lw0"))), Sem(e(nc.semaphore("lw1")))]
        mm = Sem(e(nc.semaphore("mm")))
        ev = Sem(e(nc.semaphore("ev")))
        st = Sem(e(nc.semaphore("st")))
        vz = Sem(e(nc.semaphore("vz")))
        zw = Sem(e(nc.semaphore("zw")))
        gb = Sem(e(nc.semaphore("gb")))
        vd = Sem(e(nc.semaphore("vd")))
        ws = Sem(e(nc.semaphore("ws")))
        block = e(nc.Block())

        LD_ALL = 16 * 5

        @block.sync
        def _(sync):
            ld.inc(sync.dma_start(out=cosb[:].rearrange("p (tc f) -> p tc f", f=64),
                           in_=cos0.rearrange("(tc p) f -> p tc f", p=128)), 16)
            ld.inc(sync.dma_start(
                out=sinb[:].rearrange("p (tc f) -> p tc f", f=64),
                in_=sin0.rearrange("(tc p) f -> p tc f", p=128)), 16)
            for p, (qsrc, ssrc, wdst, ngr, packw, outw, zr0) in enumerate(dplans):
                zsrc = [gz, uz, dz][p]
                ld.inc(sync.dma_start(out=zrow[p][0:ngr, 0:packw], in_=zsrc), 16)


        gb_expected = []

        @block.gpsimd
        def _(gpsimd):
            ldg.inc(gpsimd.dma_start(
                out=xb[:].rearrange("p (kc n) -> p kc n", n=TOK),
                in_=xT.rearrange("(kc p) n -> p kc n", p=128)), 16)
            for i, (w, c0, _) in enumerate(wsrc):
                if i >= 2:
                    gpsimd.wait_ge(mm.h, (i - 1) * 8)
                lw[i % 2].inc(gpsimd.dma_start(
                    out=wbuf[i % 2][:].rearrange("p (kc n) -> p kc n", n=512),
                    in_=w[:, c0:c0 + 512].rearrange("(kc p) n -> p kc n", p=128)), 16)
            gcum = 0
            for p, (qsrc, ssrc, wdst, ngr, packw, outw, zr0) in enumerate(dplans):
                gpsimd.wait_ge(vz.h, p + 1)
                zw.inc(gpsimd.dma_start(
                    out=zscr[zr0:zr0 + ngr, 0:outw], in_=zf32[0:ngr, 0:outw]), 16)
                gpsimd.wait_ge(zw.h, zw.n)
                nh = (outw + 2047) // 2048
                for g in range(ngr):
                    for h2 in range(nh):
                        c0 = h2 * 2048
                        cw = min(2048, outw - c0)
                        if gcum > 0:
                            gpsimd.wait_ge(vd.h, gcum)
                        gb.inc(gpsimd.dma_start(
                            out=zbc[:, 0:cw].unsqueeze(1),
                            in_=zscr[zr0 + g:zr0 + g + 1, c0:c0 + cw].partition_broadcast(128)), 16)
                        if h2 == 0:
                            gb.inc(gpsimd.dma_start(
                                out=qslab[:, 0:packw],
                                in_=qsrc[g * 128:(g + 1) * 128, 0:packw]), 16)
                        gb.inc(gpsimd.dma_start(
                            out=sbc[:, 0:cw].unsqueeze(1),
                            in_=ssrc[g:g + 1, c0:c0 + cw].partition_broadcast(128)), 16)
                        gb_expected.append(gb.n)
                        gcum += 1

        @block.tensor
        def _(tensor):
            tensor.wait_ge(ld.h, LD_ALL)
            tensor.wait_ge(ldg.h, 16)
            for i in range(len(wsrc)):
                tensor.wait_ge(lw[i % 2].h, 16 * (i // 2 + 1))
                for tc in range(8):
                    g = i * 8 + tc
                    if g >= 2:
                        tensor.wait_ge(ev.h, g - 1)
                    pb = ps[g % 2]
                    for kc in range(32):
                        ins = tensor.matmul(
                            pb[:],
                            xb[:, kc * TOK + tc * 128: kc * TOK + (tc + 1) * 128],
                            wbuf[i % 2][:, kc * 512:(kc + 1) * 512],
                            start=(kc == 0), stop=(kc == 31))
                    mm.inc(ins, 1)

        @block.vector
        def _(vector):
            vector.wait_ge(ld.h, LD_ALL)
            vector.wait_ge(ldg.h, 16)
            for i, (w, c0, isrope) in enumerate(wsrc):
                for tc in range(8):
                    g = i * 8 + tc
                    vector.wait_ge(mm.h, g + 1)
                    if g >= 2:
                        vector.wait_ge(st.h, 16 * g)
                    pb = ps[g % 2].ap().rearrange("p (h f) -> p h f", f=128)
                    ob = pout[g % 2].ap().rearrange("p (h f) -> p h f", f=128)
                    if isrope:
                        c_ap = cosb[:, tc * 64:(tc + 1) * 64].unsqueeze(1).to_broadcast([128, 4, 64])
                        s_ap = sinb[:, tc * 64:(tc + 1) * 64].unsqueeze(1).to_broadcast([128, 4, 64])
                        pe = pb[:, :, 0:128:2]
                        po = pb[:, :, 1:128:2]
                        oe = ob[:, :, 0:128:2]
                        oo = ob[:, :, 1:128:2]
                        t1v = t1.ap().rearrange("p (h f) -> p h f", f=64)
                        t2v = t2.ap().rearrange("p (h f) -> p h f", f=64)
                        vector.tensor_tensor(out=t1v, in0=pe, in1=c_ap, op=ALU.mult)
                        vector.tensor_tensor(out=t2v, in0=po, in1=s_ap, op=ALU.mult)
                        vector.tensor_tensor(out=oe, in0=t1v, in1=t2v, op=ALU.subtract)
                        vector.tensor_tensor(out=t1v, in0=pe, in1=s_ap, op=ALU.mult)
                        vector.tensor_tensor(out=t2v, in0=po, in1=c_ap, op=ALU.mult)
                        ins = vector.tensor_tensor(out=oo, in0=t1v, in1=t2v, op=ALU.add)
                    else:
                        ins = vector.tensor_copy(out=pout[g % 2][:], in_=ps[g % 2][:])
                    ev.inc(ins, 1)
            gcum = 0
            for p, (qsrc, ssrc, wdst, ngr, packw, outw, zr0) in enumerate(dplans):
                if p > 0:
                    vector.wait_ge(zw.h, 16 * p)
                for i in range(8):
                    vector.tensor_scalar(
                        out=zf[0:ngr, i:outw:8], in0=zrow[p][0:ngr, 0:packw],
                        scalar1=float(4 * i), scalar2=15.0,
                        op0=ALU.logical_shift_right, op1=ALU.bitwise_and)
                vz.inc(vector.tensor_copy(out=zf32[0:ngr, 0:outw],
                                          in_=zf[0:ngr, 0:outw]), 1)
                nh = (outw + 2047) // 2048
                for g in range(ngr):
                    for h2 in range(nh):
                        c0 = h2 * 2048
                        cw = min(2048, outw - c0)
                        pc0 = c0 // 8
                        pcw = cw // 8
                        vector.wait_ge(gb.h, gb_expected[gcum])
                        if gcum >= 1:
                            vector.wait_ge(ws.h, 16 * gcum)
                        for i in range(8):
                            vector.tensor_scalar(
                                out=nibi[:, i:cw:8], in0=qslab[:, pc0:pc0 + pcw],
                                scalar1=float(4 * i), scalar2=15.0,
                                op0=ALU.logical_shift_right, op1=ALU.bitwise_and)
                        vector.tensor_copy(out=nibf[:, 0:cw], in_=nibi[:, 0:cw])
                        vector.tensor_tensor(out=wslab[:, 0:cw], in0=nibf[:, 0:cw],
                                             in1=zbc[:, 0:cw], op=ALU.subtract)
                        ins = vector.tensor_tensor(out=wslab[:, 0:cw], in0=wslab[:, 0:cw],
                                                   in1=sbc[:, 0:cw], op=ALU.mult)
                        vd.inc(ins, 1)
                        gcum += 1

        @block.scalar
        def _(scalar):
            for i in range(len(wsrc)):
                for tc in range(8):
                    g = i * 8 + tc
                    scalar.wait_ge(ev.h, g + 1)
                    dst, c0 = outmap[i]
                    st.inc(scalar.dma_start(
                        out=dst[tc * 128:(tc + 1) * 128, c0:c0 + 512],
                        in_=pout[g % 2][:]), 16)
            gcum = 0
            for p, (qsrc, ssrc, wdst, ngr, packw, outw, zr0) in enumerate(dplans):
                nh = (outw + 2047) // 2048
                for g in range(ngr):
                    for h2 in range(nh):
                        c0 = h2 * 2048
                        cw = min(2048, outw - c0)
                        scalar.wait_ge(vd.h, gcum + 1)
                        ws.inc(scalar.dma_start(
                            out=wdst[g * 128:(g + 1) * 128, c0:c0 + cw],
                            in_=wslab[:, 0:cw]), 16)
                        gcum += 1
            scalar.wait_ge(st.h, st.n)
            scalar.wait_ge(ws.h, ws.n)

    return nc



def rope0_tables():
    freqs = 1.0 / (10000.0 ** (np.arange(0, HD, 2, dtype=np.float32) / HD))
    t = np.arange(S, dtype=np.float32)
    ang = np.outer(t, freqs)
    return np.cos(ang).astype(np.float32), np.sin(ang).astype(np.float32)


def rope1_tables():
    inv = 1.0 / (10000.0 ** (np.arange(0, HD, 2, dtype=np.float32) / HD))
    t = np.arange(S, dtype=np.float32)
    fr = np.outer(t, inv)
    emb = np.concatenate([fr, fr], axis=1)
    return np.cos(emb).astype(np.float32), np.sin(emb).astype(np.float32)


def launch_A_inputs(inputs):
    x = np.asarray(inputs["L0_attn_x"], np.float32).reshape(B * S, D)
    wq = np.ascontiguousarray(np.asarray(inputs["L0_attn_wq"], np.float32))
    wk = np.ascontiguousarray(np.asarray(inputs["L0_attn_wk"], np.float32))
    wv = np.ascontiguousarray(np.asarray(inputs["L0_attn_wv"], np.float32))
    cos0, sin0 = rope0_tables()
    gq = np.asarray(inputs["L1_ffn_gate_qweight"], np.int32)
    gz = np.asarray(inputs["L1_ffn_gate_qzeros"], np.int32)
    gs = np.asarray(inputs["L1_ffn_gate_scales"], np.float32)
    uq = np.asarray(inputs["L1_ffn_up_qweight"], np.int32)
    uz = np.asarray(inputs["L1_ffn_up_qzeros"], np.int32)
    us = np.asarray(inputs["L1_ffn_up_scales"], np.float32)
    dq = np.asarray(inputs["L1_ffn_down_qweight"], np.int32)
    dz = np.asarray(inputs["L1_ffn_down_qzeros"], np.int32)
    dsc = np.asarray(inputs["L1_ffn_down_scales"], np.float32)
    dq_p = np.zeros((88 * 128, 512), np.int32)
    dq_p[0:I_FFN] = dq
    dz_p = np.zeros((88, 512), np.int32)
    dz_p[0:86] = dz
    ds_p = np.zeros((88, D), np.float32)
    ds_p[0:86] = dsc

    maps = []
    for c in range(NCORES):
        t0 = c * TOK
        xs = np.ascontiguousarray(x[t0:t0 + TOK].T)
        tok0 = (c % 4) * TOK
        m = dict(
            xT=xs, wq=wq, wk=wk, wv=wv,
            cos0=np.ascontiguousarray(cos0[tok0:tok0 + TOK]),
            sin0=np.ascontiguousarray(sin0[tok0:tok0 + TOK]),
            gq=np.ascontiguousarray(gq[:, c * 172:(c + 1) * 172]),
            gz=np.ascontiguousarray(gz[:, c * 172:(c + 1) * 172]),
            gs=np.ascontiguousarray(gs[:, c * 1376:(c + 1) * 1376]),
            uq=np.ascontiguousarray(uq[:, c * 172:(c + 1) * 172]),
            uz=np.ascontiguousarray(uz[:, c * 172:(c + 1) * 172]),
            us=np.ascontiguousarray(us[:, c * 1376:(c + 1) * 1376]),
            dqw=np.ascontiguousarray(dq_p[c * NG_D * 128:(c + 1) * NG_D * 128]),
            dz=np.ascontiguousarray(dz_p[c * NG_D:(c + 1) * NG_D]),
            ds=np.ascontiguousarray(ds_p[c * NG_D:(c + 1) * NG_D]),
        )
        maps.append(m)
    return maps


def kernel(**inputs):
    raise NotImplementedError("built incrementally; see launches")




class Plan:
    ENGS = ("sync", "gpsimd", "tensor", "vector", "scalar")

    def __init__(self, nc):
        self.nc = nc
        self.ops = {k: [] for k in self.ENGS}
        self.sems = {}
        self.cnt = {}

    def sem(self, ctx, name):
        h = ctx.enter_context(self.nc.semaphore(name))
        self.sems[name] = h
        self.cnt[name] = 0
        return h

    ENG_SEM = {"tensor": "pe", "vector": "dv", "scalar": "ac"}

    def add(self, eng, fn, waits=(), inc=None, inck=1):
        waits = list(waits)
        es = self.ENG_SEM.get(eng)
        if es is not None and es in self.cnt:
            if eng != "tensor":
                waits.append((es, self.cnt[es]))
            if inc is None:
                inc = es
        self.ops[eng].append((fn, waits, inc, inck))
        if inc is not None:
            self.cnt[inc] += inck
            return self.cnt[inc]
        return None

    def check(self):
        idx = {k: 0 for k in self.ENGS}
        cur = {k: 0 for k in self.cnt}
        total = len(sum(self.ops.values(), []))
        done = 0
        progress = True
        while progress:
            progress = False
            for eng in self.ENGS:
                while idx[eng] < len(self.ops[eng]):
                    fn, waits, inc, inck = self.ops[eng][idx[eng]]
                    if all(cur[sn] >= v for sn, v in waits):
                        if inc is not None:
                            cur[inc] += inck
                        idx[eng] += 1
                        done += 1
                        progress = True
                    else:
                        break
        if done < total:
            for eng in self.ENGS:
                if idx[eng] < len(self.ops[eng]):
                    fn, waits, inc, inck = self.ops[eng][idx[eng]]
                    bad = [(sn, v, cur[sn]) for sn, v in waits if cur[sn] < v]
                    print(f"DEADLOCK {eng} op#{idx[eng]}: needs {bad}")
            raise AssertionError(f"plan deadlock: {done}/{total} ops runnable")
        return True

    def emit(self, block):
        for eng in self.ENGS:
            if not self.ops[eng]:
                continue
            ops = self.ops[eng]
            sems = self.sems

            def make(eng_ops):
                def body(engine):
                    for fn, waits, inc, inck in eng_ops:
                        for sname, val in waits:
                            engine.wait_ge(sems[sname], val)
                        ins = fn(engine)
                        if inc is not None:
                            ins.then_inc(sems[inc], inck)
                return body
            getattr(block, eng)(make(ops))



def emit_chunk_pv(P, jh, ci, nw, u, nk, tr_i, ready_val, ready_sem,
                  ev_pte, ev_pvk, ev_tr_last, ev_on, pbuf, ptp, ptsb, idb, vb,
                  psO):
    for jj in range(nw // 128):
        kc = ci * 4 + jj
        off = jj * 128
        wl = [(ready_sem, ready_val)]
        if tr_i >= 2:
            wl.append(("ac", ev_pte[tr_i - 2]))

        def trans(E, ci=ci, off=off, t=tr_i):
            return E.transpose(
                ptp[t % 2][:, 0:128], pbuf[ci % 2][:, off:off + 128], idb[:])
        P.add("tensor", trans, waits=wl, inc="pe")
        ev_tr_last[(jh, ci)] = P.cnt["pe"]

        wl = [("pe", P.cnt["pe"])]
        if tr_i >= 2:
            wl.append(("pe", ev_pvk[tr_i - 2]))

        def ptevac(E, t=tr_i):
            return E.activation(out=ptsb[t % 2][:], in_=ptp[t % 2][:, 0:128],
                                func=AF.Copy)
        P.add("scalar", ptevac, waits=wl, inc="ac")
        ev_pte[tr_i] = P.cnt["ac"]

        wl = [("ac", ev_pte[tr_i])]
        if kc == 0 and jh >= 2:
            wl.append(("dv", ev_on[jh - 2]))

        def pvmm(E, kc=kc, u=u, t=tr_i, jh=jh, nk=nk):
            return E.matmul(
                psO[jh % 2][:, 0:128], ptsb[t % 2][:],
                vb[:, kc * 256 + u * 128:kc * 256 + (u + 1) * 128],
                start=(kc == 0), stop=(kc == nk - 1))
        P.add("tensor", pvmm, waits=wl, inc="pe")
        ev_pvk[tr_i] = P.cnt["pe"]
        tr_i += 1
    return tr_i



def build_launch_attn(layer, jobs_override=None):
    nc = bass.Bass("TRN2", target_bir_lowering=False, debug=False)
    qT = nc.dram_tensor("qT", [128, 8 * S], BF16, kind="ExternalInput").ap()
    kT = nc.dram_tensor("kT", [128, 2 * S], BF16, kind="ExternalInput").ap()
    vt = nc.dram_tensor("vt", [S, 2 * HD], BF16, kind="ExternalInput").ap()
    idn = nc.dram_tensor("idn", [128, 128], BF16, kind="ExternalInput").ap()
    tri = nc.dram_tensor("tri", [128, 128], F32, kind="ExternalInput").ap()
    trit = nc.dram_tensor("trit", [128, 128], F32, kind="ExternalInput").ap()
    cst = nc.dram_tensor("cst", [128, 8], F32, kind="ExternalInput").ap()
    onesc = nc.dram_tensor("onesc", [128, 1], BF16, kind="ExternalInput").ap()
    negr = nc.dram_tensor("negr", [1, 128], F32, kind="ExternalInput").ap()
    idf = nc.dram_tensor("idf", [128, 128], F32, kind="ExternalInput").ap()
    onfr = nc.dram_tensor("onfr", [1, 128], F32, kind="ExternalInput").ap()
    oT = nc.dram_tensor("oT", [1024, S], BF16, kind="ExternalOutput").ap()
    if layer == 1:
        tgt = nc.dram_tensor("tgt", [128, 32], F32, kind="ExternalInput").ap()
        tgt4 = nc.dram_tensor("tgt4", [128, 32], F32, kind="ExternalInput").ap()
        tgt2 = nc.dram_tensor("tgt2", [128, 32], F32, kind="ExternalInput").ap()
        tg2 = nc.dram_tensor("tg2", [128, 32], F32, kind="ExternalInput").ap()
        fmul = nc.dram_tensor("fmul", [128, 32], F32, kind="ExternalInput").ap()
        fadd = nc.dram_tensor("fadd", [128, 32], F32, kind="ExternalInput").ap()
        faddn = nc.dram_tensor("faddn", [128, 32], F32, kind="ExternalInput").ap()
        stepr = nc.dram_tensor("stepr", [128, 24], F32, kind="ExternalInput").ap()

    NQT = 32
    NITER = 14
    B0 = 16.0

    with ExitStack() as ctx:
        e = ctx.enter_context
        kb = e(nc.sbuf_tensor("kb", [128, 2 * S], BF16))
        vb = e(nc.sbuf_tensor("vb", [128, 32 * 2 * HD], BF16))
        qt = [e(nc.sbuf_tensor(f"qt{i}", [128, 512], BF16)) for i in range(2)]
        idb = e(nc.sbuf_tensor("idb", [128, 128], BF16))
        trib = e(nc.sbuf_tensor("trib", [128, 128], F32))
        tritb = e(nc.sbuf_tensor("tritb", [128, 128], F32))
        tribb = e(nc.sbuf_tensor("tribb", [128, 128], BF16))
        cstb = e(nc.sbuf_tensor("cstb", [128, 8], F32))
        on1 = e(nc.sbuf_tensor("on1", [128, 1], BF16))
        ngr_ = e(nc.sbuf_tensor("ngr", [1, 128], F32))
        idfb = e(nc.sbuf_tensor("idfb", [128, 128], F32))
        onf = e(nc.sbuf_tensor("onf", [1, 128], F32))
        pbuf = [e(nc.sbuf_tensor(f"pbuf{i}", [128, 512], BF16)) for i in range(2)]
        zrow = e(nc.sbuf_tensor("zrow", [1, 512], F32))
        zir = e(nc.sbuf_tensor("zir", [1, 512], F32))
        osb = e(nc.sbuf_tensor("osb", [128, 512], BF16))
        onrm = e(nc.sbuf_tensor("onrm", [128, 512], BF16))
        if layer == 1:
            sq = [e(nc.sbuf_tensor(f"sq{i}", [128, S], F32)) for i in range(4)]
            ebuf = [e(nc.sbuf_tensor(f"ebuf{i}", [128, 512], BF16)) for i in range(2)]
            junkb = e(nc.sbuf_tensor("junkb", [128, S], BF16))
            tgtb = e(nc.sbuf_tensor("tgtb", [128, 32], F32))
            tgt4b = e(nc.sbuf_tensor("tgt4b", [128, 32], F32))
            tgt2b = e(nc.sbuf_tensor("tgt2b", [128, 32], F32))
            tg2b = e(nc.sbuf_tensor("tg2b", [128, 32], F32))
            fmulb = e(nc.sbuf_tensor("fmulb", [128, 32], F32))
            faddb = e(nc.sbuf_tensor("faddb", [128, 32], F32))
            faddnb = e(nc.sbuf_tensor("faddnb", [128, 32], F32))
            steprb = e(nc.sbuf_tensor("steprb", [128, 24], F32))
            thr = e(nc.sbuf_tensor("thr", [128, 4], F32))
            thr2 = e(nc.sbuf_tensor("thr2", [128, 4], F32))
            thr2b = e(nc.sbuf_tensor("thr2b", [128, 1], F32))
            junkb2 = e(nc.sbuf_tensor("junkb2", [128, S], BF16))
            trow = e(nc.sbuf_tensor("trow", [1, 512], F32))

        psST = [e(nc.psum_tensor(f"psST{i}", [128, 512], F32)) for i in range(2)]
        psO = e(nc.psum_tensor("psO", [128, 512], F32))
        psZ = e(nc.psum_tensor("psZ", [128, 512], F32))
        psIZ = e(nc.psum_tensor("psIZ", [128, 512], F32))
        if layer == 1:
            psSq = [e(nc.psum_tensor(f"psSq{i}", [128, 512], F32)) for i in range(2)]
            psTR = e(nc.psum_tensor("psTR", [128, 512], F32))

        P = Plan(nc)
        for sname in ("ld", "lq0", "lq1", "pe", "ac", "dv", "st"):
            P.sem(ctx, sname)

        jobs = [(u, g) for u in range(2) for g in range(NQT)]
        if jobs_override is not None:
            jobs = jobs_override

        P.add("sync", lambda E: E.dma_start(out=kb[:], in_=kT), inc="ld", inck=16)
        P.add("sync", lambda E: E.dma_start(
            out=vb[:].rearrange("p (kc n) -> p kc n", n=2 * HD),
            in_=vt.rearrange("(kc p) n -> p kc n", p=128)), inc="ld", inck=16)
        P.add("sync", lambda E: E.dma_start(out=idb[:], in_=idn), inc="ld", inck=16)
        P.add("sync", lambda E: E.dma_start(out=trib[:], in_=tri), inc="ld", inck=16)
        P.add("sync", lambda E: E.dma_start(out=tritb[:], in_=trit), inc="ld", inck=16)
        P.add("sync", lambda E: E.dma_start(out=cstb[:], in_=cst), inc="ld", inck=16)
        P.add("sync", lambda E: E.dma_start(out=on1[:], in_=onesc), inc="ld", inck=16)
        P.add("sync", lambda E: E.dma_start(out=ngr_[:], in_=negr), inc="ld", inck=16)
        P.add("sync", lambda E: E.dma_start(out=idfb[:], in_=idf), inc="ld", inck=16)
        P.add("sync", lambda E: E.dma_start(out=onf[:], in_=onfr), inc="ld", inck=16)
        if layer == 1:
            P.add("sync", lambda E: E.dma_start(out=tgtb[:], in_=tgt), inc="ld", inck=16)
            P.add("sync", lambda E: E.dma_start(out=tgt4b[:], in_=tgt4), inc="ld", inck=16)
            P.add("sync", lambda E: E.dma_start(out=tgt2b[:], in_=tgt2), inc="ld", inck=16)
            P.add("sync", lambda E: E.dma_start(out=tg2b[:], in_=tg2), inc="ld", inck=16)
            P.add("sync", lambda E: E.dma_start(out=fmulb[:], in_=fmul), inc="ld", inck=16)
            P.add("sync", lambda E: E.dma_start(out=faddb[:], in_=fadd), inc="ld", inck=16)
            P.add("sync", lambda E: E.dma_start(out=faddnb[:], in_=faddn), inc="ld", inck=16)
            P.add("sync", lambda E: E.dma_start(out=steprb[:], in_=stepr), inc="ld", inck=16)
        LD_ALL = P.cnt["ld"]
        if layer == 1:
            P.add("vector", lambda E: E.tensor_copy(out=tribb[:], in_=trib[:]),
                  waits=[("ld", LD_ALL)])

        def plan_qload(j, waits):
            u, g = jobs[j]

            def qload(E, u=u, g=g, j=j):
                return E.dma_start(
                    out=qt[j % 2][:].rearrange("p (h t) -> p h t", t=128),
                    in_=qT.rearrange("p (h t) -> p h t", t=S)
                    [:, u * 4:(u + 1) * 4, g * 128:(g + 1) * 128])
            P.add("sync", qload, waits=waits, inc=f"lq{j % 2}", inck=16)

        plan_qload(0, [])
        if len(jobs) > 1:
            plan_qload(1, [])

        ev_sqev = []
        prev_cnt_dv = 0
        prev_cnt_ac = 0
        ev_pP = {}
        ev_pcons = {}
        ev_expc = []
        last_oevac = 0
        last_zevac = 0
        last_izmul = 0
        last_store = 0
        pgi = 0

        for j, (u, g) in enumerate(jobs):
            ext = (g + 1) * 128
            nkc = g + 1
            ngrp = (ext + 511) // 512
            first_wait = [(f"lq{j % 2}", 16 * (j // 2 + 1))]
            if j == 0:
                first_wait.append(("ld", LD_ALL))

            if layer == 1:
                sqi = 0
                ev_tria = {}
                for hl in range(4):
                    for ci in range(ngrp):
                        n0 = ci * 512
                        nw = min(512, ext - n0)
                        wl = list(first_wait) if (hl == 0 and ci == 0) else []
                        if len(ev_sqev) >= 2:
                            wl.append(ev_sqev[-2])
                        if j >= 1 and hl == 0 and ci == 0:
                            wl.append(("dv", prev_cnt_dv))
                            wl.append(("ac", prev_cnt_ac))

                        def mmsq(E, j=j, u=u, hl=hl, n0=n0, nw=nw, par=sqi % 2):
                            return E.matmul(
                                psSq[par][:, 0:nw],
                                qt[j % 2][:, hl * 128:(hl + 1) * 128],
                                kb[:, u * S + n0:u * S + n0 + nw],
                                start=True, stop=True)
                        P.add("tensor", mmsq, waits=wl)
                        mm_pe = P.cnt["pe"]

                        if hl % 2 == 0:
                            def sqevd(E, n0=n0, nw=nw, par=sqi % 2, hl=hl):
                                return E.tensor_copy(
                                    out=sq[hl][:, n0:n0 + nw],
                                    in_=psSq[par][:, 0:nw])
                            P.add("vector", sqevd, waits=[("pe", mm_pe)])
                            ev_sqev.append(("dv", P.cnt["dv"]))
                        else:
                            def sqeva(E, n0=n0, nw=nw, par=sqi % 2, hl=hl):
                                return E.activation(
                                    out=sq[hl][:, n0:n0 + nw],
                                    in_=psSq[par][:, 0:nw], func=AF.Copy)
                            P.add("scalar", sqeva, waits=[("pe", mm_pe)])
                            ev_sqev.append(("ac", P.cnt["ac"]))
                        sqi += 1
                    last_ev = ev_sqev[-1]

                    def tria(E, ext=ext, hl=hl):
                        return E.tensor_tensor(
                            out=sq[hl][:, ext - 128:ext],
                            in0=sq[hl][:, ext - 128:ext],
                            in1=trib[:], op=ALU.add)
                    P.add("vector", tria, waits=[last_ev])
                    ev_tria[hl] = P.cnt["dv"]

                th_ready = {}
                for hl in (1, 3):
                    P.add("scalar", lambda E: E.memzero(thr2[:, 0:1]),
                          waits=[("dv", ev_tria[hl])])
                    for it in range(NITER):
                        def cnta(E, ext=ext, hl=hl):
                            return E.activation(
                                out=junkb[:, 0:ext], in_=sq[hl][:, 0:ext],
                                func=AF.Sign, bias=thr2[:, 0:1], scale=1.0,
                                accum_out=thr2[:, 1:2])
                        P.add("scalar", cnta)

                        def sgna(E, g=g):
                            return E.activation(
                                out=thr2[:, 2:3], in_=thr2[:, 1:2],
                                func=AF.Sign, bias=tg2b[:, g:g + 1],
                                scale=-1.0)
                        P.add("scalar", sgna)

                        def mula(E, it=it):
                            return E.mul(thr2[:, 2:3], thr2[:, 2:3],
                                         steprb[:, it:it + 1])
                        P.add("scalar", mula)

                        def adda(E):
                            return E.add(thr2[:, 0:1], thr2[:, 0:1],
                                         thr2[:, 2:3])
                        P.add("scalar", adda)

                    def tfina(E):
                        return E.add(thr2[:, 0:1], thr2[:, 0:1],
                                     steprb[:, NITER:NITER + 1])
                    P.add("scalar", tfina)

                    def tforcea(E, g=g):
                        return E.mul(thr2[:, 0:1], thr2[:, 0:1],
                                     fmulb[:, g:g + 1])
                    P.add("scalar", tforcea)

                    def tforceb(E, g=g):
                        return E.add(thr2[:, 0:1], thr2[:, 0:1],
                                     faddnb[:, g:g + 1])
                    P.add("scalar", tforceb)

                    def nega(E, hl=hl):
                        return E.mul(thr2[:, 3:4] if hl == 1 else thr2[:, 2:3],
                                     thr2[:, 0:1], cstb[:, 2:3])
                    P.add("scalar", nega)
                    th_ready[hl] = ("ac", P.cnt["ac"],
                                    thr2[:, 3:4] if hl == 1 else thr2[:, 2:3])
                for hl in (0, 2):
                    P.add("vector", lambda E: E.memset(thr[:, 0:1], 0.0),
                          waits=[("dv", ev_tria[hl])])
                    for it in range(NITER):
                        step = B0 * (0.5 ** it)

                        def cnt_(E, ext=ext, hl=hl):
                            return E.tensor_scalar(
                                out=junkb2[:, 0:ext], in0=sq[hl][:, 0:ext],
                                scalar1=thr[:, 0:1], scalar2=None,
                                op0=ALU.is_ge, op1=ALU.add,
                                accum_out=thr[:, 1:2])
                        P.add("vector", cnt_)

                        def dlt(E, g=g, step=step):
                            return E.tensor_scalar(
                                out=thr[:, 2:3], in0=thr[:, 1:2],
                                scalar1=tgtb[:, g:g + 1], scalar2=2.0 * step,
                                op0=ALU.is_ge, op1=ALU.mult)
                        P.add("vector", dlt)

                        def upd(E, step=step):
                            return E.scalar_tensor_tensor(
                                out=thr[:, 0:1], in0=thr[:, 2:3],
                                scalar=-step, in1=thr[:, 0:1],
                                op0=ALU.add, op1=ALU.add)
                        P.add("vector", upd)
                    last_step = B0 * (0.5 ** (NITER - 1))

                    def tfin(E, last_step=last_step):
                        return E.tensor_scalar(
                            out=thr[:, 0:1], in0=thr[:, 0:1],
                            scalar1=-3.0 * last_step, scalar2=None,
                            op0=ALU.add)
                    P.add("vector", tfin)

                    def tforce(E, g=g):
                        return E.scalar_tensor_tensor(
                            out=thr[:, 0:1], in0=thr[:, 0:1],
                            scalar=fmulb[:, g:g + 1], in1=faddb[:, g:g + 1],
                            op0=ALU.mult, op1=ALU.add)
                    P.add("vector", tforce)

                    def tsave(E, hl=hl):
                        return E.tensor_copy(
                            out=thr[:, 3:4] if hl == 0 else thr2b[:, 0:1],
                            in_=thr[:, 0:1])
                    P.add("vector", tsave)
                    th_ready[hl] = ("dv", P.cnt["dv"],
                                    thr[:, 3:4] if hl == 0 else thr2b[:, 0:1])
                prev_cnt_dv = P.cnt["dv"]
                prev_cnt_ac = P.cnt["ac"]

                last_tev = []
                for hl in range(4):
                    sem, val, col = th_ready[hl]

                    def ttr(E, col=col):
                        return E.transpose(psTR[0:1, 0:128], col, idfb[:])
                    P.add("tensor", ttr, waits=[(sem, val)] + last_tev)
                    ttr_pe = P.cnt["pe"]

                    def tev(E, hl=hl):
                        return E.activation(
                            out=trow[0:1, hl * 128:(hl + 1) * 128],
                            in_=psTR[0:1, 0:128], func=AF.Copy)
                    P.add("scalar", tev, waits=[("pe", ttr_pe)])
                    last_tev = [("ac", P.cnt["ac"])]
                trow_ac = P.cnt["ac"]

            for p in range(nkc):
                par = pgi % 2
                wl = []
                if layer == 0 and p == 0:
                    wl += list(first_wait) if (j == 0 or True) else []
                if layer == 1 and p == 0:
                    wl.append(("ac", trow_ac))
                if pgi >= 2:
                    wl.append(("ac", ev_expc[pgi - 2]))

                def mmst(E, j=j, u=u, p=p, par=par):
                    return E.matmul(
                        psST[par][:], kb[:, u * S + p * 128:u * S + (p + 1) * 128],
                        qt[j % 2][:], start=True, stop=(layer == 0))
                P.add("tensor", mmst, waits=wl)
                if layer == 1:
                    def mmt(E, par=par):
                        return E.matmul(
                            psST[par][:], ngr_[:], trow[:],
                            start=False, stop=True)
                    P.add("tensor", mmt)
                mm_pe = P.cnt["pe"]

                exp_wl = [("pe", mm_pe)]
                if p == nkc - 1:
                    def triST(E, par=par):
                        return E.tensor_tensor(
                            out=psST[par][:].rearrange("p (h q) -> p h q", q=128),
                            in0=psST[par][:].rearrange("p (h q) -> p h q", q=128),
                            in1=tritb[:].unsqueeze(1).to_broadcast([128, 4, 128]),
                            op=ALU.add)
                    P.add("vector", triST, waits=[("pe", mm_pe)])
                    exp_wl = [("dv", P.cnt["dv"])]

                if layer == 0:
                    if pgi >= 2:
                        exp_wl.append(("pe", ev_pcons[pgi - 2]))

                    def expP(E, par=par):
                        return E.activation(
                            out=pbuf[par][:], in_=psST[par][:], func=AF.Exp,
                            bias=cstb[:, 0:1], scale=cstb[:, 1:2])
                    P.add("scalar", expP, waits=exp_wl)
                    ev_pP[pgi] = ("ac", P.cnt["ac"])
                    ev_expc.append(P.cnt["ac"])
                else:
                    def expE(E, par=par):
                        return E.activation(
                            out=ebuf[par][:], in_=psST[par][:], func=AF.Exp,
                            bias=cstb[:, 0:1], scale=cstb[:, 1:2])
                    P.add("scalar", expE, waits=exp_wl)
                    e_ac = P.cnt["ac"]

                    stt_wl = [("ac", e_ac)]
                    if pgi >= 2:
                        stt_wl.append(("pe", ev_pcons[pgi - 2]))

                    def sttP(E, par=par):
                        return E.scalar_tensor_tensor(
                            out=pbuf[par][:], in0=psST[par][:], scalar=0.0,
                            in1=ebuf[par][:], op0=ALU.is_ge, op1=ALU.mult)
                    P.add("vector", sttP, waits=stt_wl)
                    ev_pP[pgi] = ("dv", P.cnt["dv"])
                    ev_expc.append(P.cnt["ac"])

                pv_wl = [ev_pP[pgi]]
                if p == 0:
                    pv_wl.append(("ac", last_oevac))

                def mmpv(E, u=u, p=p, par=par, nkc=nkc):
                    return E.matmul(
                        psO[:], vb[:, p * 256 + u * 128:p * 256 + (u + 1) * 128],
                        pbuf[par][:], start=(p == 0), stop=(p == nkc - 1))
                P.add("tensor", mmpv, waits=pv_wl)

                z_wl = []
                if p == 0:
                    z_wl.append(("dv", last_zevac))

                def mmz(E, par=par, p=p, nkc=nkc):
                    return E.matmul(
                        psZ[0:1, :], on1[:], pbuf[par][:],
                        start=(p == 0), stop=(p == nkc - 1))
                P.add("tensor", mmz, waits=z_wl)
                ev_pcons[pgi] = P.cnt["pe"]
                pgi += 1
            pv_done = P.cnt["pe"]

            def zev(E):
                return E.tensor_scalar(
                    out=zrow[:], in0=psZ[0:1, :], scalar1=1.0, scalar2=None,
                    op0=ALU.mult)
            P.add("vector", zev, waits=[("pe", pv_done)])
            last_zevac = P.cnt["dv"]
            P.add("vector", lambda E: E.reciprocal(out=zir[:], in_=zrow[:]))
            zir_dv = P.cnt["dv"]

            def bcz(E):
                return E.matmul(psIZ[:], onf[:], zir[:], start=True, stop=True)
            P.add("tensor", bcz, waits=[("dv", zir_dv),
                                        ("dv", last_izmul)])
            bcz_pe = P.cnt["pe"]

            def oev(E):
                return E.activation(out=osb[:], in_=psO[:], func=AF.Copy)
            P.add("scalar", oev, waits=[("pe", pv_done), ("dv", P.cnt["dv"])])
            last_oevac = P.cnt["ac"]

            def omul(E):
                return E.tensor_tensor(out=onrm[:], in0=osb[:], in1=psIZ[:],
                                       op=ALU.mult)
            P.add("vector", omul, waits=[("ac", last_oevac), ("pe", bcz_pe),
                                         ("st", last_store)])
            last_izmul = P.cnt["dv"]
            om_dv = P.cnt["dv"]

            def ost(E, u=u, g=g):
                return E.dma_start(
                    out=oT.rearrange("(h p) t -> p h t", p=128)
                    [:, u * 4:(u + 1) * 4, g * 128:(g + 1) * 128],
                    in_=onrm[:].rearrange("p (h q) -> p h q", q=128))
            P.add("scalar", ost, waits=[("dv", om_dv)], inc="st", inck=16)
            last_store = P.cnt["st"]

            if j + 2 < len(jobs):
                plan_qload(j + 2, [("pe", P.cnt["pe"])])

        P.ops["scalar"].append(
            (lambda E: E.wait_ge(P.sems["st"], P.cnt["st"]), [], None, 1))
        P.check()
        block = e(nc.Block())
        P.emit(block)
    return nc


def launch_attn_inputs(layer, q, k, v):
    cst = np.zeros((128, 8), np.float32)
    cst[:, 0] = -20.0
    cst[:, 1] = 1.0 / math.sqrt(HD)
    cst[:, 2] = -1.0
    idn = np.eye(128, dtype=BF)
    tri = np.triu(np.full((128, 128), -1e30, np.float32), k=1)
    trit = np.tril(np.full((128, 128), -1e30, np.float32), k=-1)
    onesc = np.ones((128, 1), BF)
    negr = np.full((1, 128), -1.0, np.float32)
    idf = np.eye(128, dtype=np.float32)
    onfr = np.ones((1, 128), np.float32)
    maps = []
    for c in range(NCORES):
        b = c // 4
        kv0 = 2 * (c % 4)
        h0 = 8 * (c % 4)
        qsl = q[b * S:(b + 1) * S, h0 * HD:(h0 + 8) * HD]
        qTl = np.ascontiguousarray(qsl.T.reshape(8, HD, S).transpose(1, 0, 2)
                                   .reshape(HD, 8 * S))
        ksl = k[b * S:(b + 1) * S, kv0 * HD:(kv0 + 2) * HD]
        kTl = np.ascontiguousarray(ksl.T.reshape(2, HD, S).transpose(1, 0, 2)
                                   .reshape(HD, 2 * S))
        vtl = np.ascontiguousarray(v[b * S:(b + 1) * S, kv0 * HD:(kv0 + 2) * HD])
        m = dict(qT=qTl, kT=kTl, vt=vtl, idn=idn, tri=tri, trit=trit, cst=cst,
                 onesc=onesc, negr=negr, idf=idf, onfr=onfr)
        if layer == 1:
            NITER = 14
            B0 = 16.0
            tgt = np.zeros((128, 32), np.float32)
            tg2 = np.zeros((128, 32), np.float32)
            fmul = np.zeros((128, 32), np.float32)
            fadd = np.zeros((128, 32), np.float32)
            for g in range(32):
                nval = g * 128 + np.arange(128) + 1
                ext = (g + 1) * 128
                tgt[:, g] = np.minimum(410, nval)
                tg2[:, g] = 2.0 * tgt[:, g] - ext
                keep_all = nval <= 410
                fmul[:, g] = np.where(keep_all, 0.0, 1.0)
                fadd[:, g] = np.where(keep_all, -64.0, 0.0)
            stepr = np.zeros((128, 24), np.float32)
            for it in range(NITER):
                stepr[:, it] = B0 * (0.5 ** it)
            stepr[:, NITER] = 3.0 * B0 * (0.5 ** (NITER - 1))
            m.update(tgt=tgt, tgt4=tgt / 4.0, tgt2=tgt / 2.0, tg2=tg2,
                     fmul=fmul, fadd=fadd, faddn=-fadd, stepr=stepr)
        maps.append(m)
    return maps



def build_launch_C(phases=3):
    nc = bass.Bass("TRN2", target_bir_lowering=False, debug=False)
    x0T = nc.dram_tensor("x0T", [D, TOK], F32, kind="ExternalInput").ap()
    oTm = nc.dram_tensor("oTm", [D, TOK], BF16, kind="ExternalInput").ap()
    wo0 = nc.dram_tensor("wo0", [128, D * D // 128], F32, kind="ExternalInput").ap()
    Wb = nc.dram_tensor("Wb", [128, D * D // 128], F32, kind="ExternalInput").ap()
    bbr = nc.dram_tensor("bbr", [128, 32], F32, kind="ExternalInput").ap()
    Wr = nc.dram_tensor("Wr", [D, 8], F32, kind="ExternalInput").ap()
    brr = nc.dram_tensor("brr", [1, 8], F32, kind="ExternalInput").ap()
    Aq = nc.dram_tensor("Aq", [D, 128], F32, kind="ExternalInput").ap()
    Bq = nc.dram_tensor("Bq", [128, D], F32, kind="ExternalInput").ap()
    wq1 = nc.dram_tensor("wq1", [128, D * D // 128], F32, kind="ExternalInput").ap()
    wk1 = nc.dram_tensor("wk1", [128, D * NKV * HD // 128], F32, kind="ExternalInput").ap()
    wv1 = nc.dram_tensor("wv1", [128, D * NKV * HD // 128], F32, kind="ExternalInput").ap()
    cos1 = nc.dram_tensor("cos1", [TOK, HD], F32, kind="ExternalInput").ap()
    sin1 = nc.dram_tensor("sin1", [TOK, HD], F32, kind="ExternalInput").ap()
    iota8 = nc.dram_tensor("iota8", [128, 8], F32, kind="ExternalInput").ap()
    ones8 = nc.dram_tensor("ones8", [128, 8], F32, kind="ExternalInput").ap()
    onesb = nc.dram_tensor("onesb", [128, 1], BF16, kind="ExternalInput").ap()
    onesr = nc.dram_tensor("onesr", [1, 128], BF16, kind="ExternalInput").ap()
    onesf = nc.dram_tensor("onesf", [1, 512], F32, kind="ExternalInput").ap()
    epsr = nc.dram_tensor("epsr", [1, 1], F32, kind="ExternalInput").ap()
    idn = nc.dram_tensor("idn", [128, 128], BF16, kind="ExternalInput").ap()

    x2To = nc.dram_tensor("x2T", [D, TOK], F32, kind="ExternalOutput").ap()
    q1o = nc.dram_tensor("q1o", [TOK, D], BF16, kind="ExternalOutput").ap()
    k1o = nc.dram_tensor("k1o", [TOK, NKV * HD], BF16, kind="ExternalOutput").ap()
    v1o = nc.dram_tensor("v1o", [TOK, NKV * HD], BF16, kind="ExternalOutput").ap()

    NT = 512

    with ExitStack() as ctx:
        e = ctx.enter_context
        rbuf = e(nc.sbuf_tensor("rbuf", [128, 32 * NT], BF16))
        x1buf = e(nc.sbuf_tensor("x1buf", [128, 32 * NT], BF16))
        wtile = [e(nc.sbuf_tensor(f"wtile{i}", [128, 32 * 128], BF16)) for i in range(2)]
        wbig = [e(nc.sbuf_tensor(f"wbig{i}", [128, 32 * 512], BF16)) for i in range(2)]
        x0c = [e(nc.sbuf_tensor(f"x0c{i}", [128, 512], F32)) for i in range(2)]
        bqb = e(nc.sbuf_tensor("bqb", [128, D], BF16))
        wrb = e(nc.sbuf_tensor("wrb", [128, 32 * 8], BF16))
        brb = e(nc.sbuf_tensor("brb", [1, 8], BF16))
        bbt = e(nc.sbuf_tensor("bbt", [128, 32], F32))
        io8 = e(nc.sbuf_tensor("io8", [128, 8], F32))
        on8 = e(nc.sbuf_tensor("on8", [128, 8], F32))
        on1 = e(nc.sbuf_tensor("on1", [128, 1], BF16))
        onr = e(nc.sbuf_tensor("onr", [1, 128], BF16))
        onf = e(nc.sbuf_tensor("onf", [1, 512], F32))
        epsb = e(nc.sbuf_tensor("epsb", [1, 1], F32))
        idb = e(nc.sbuf_tensor("idb", [128, 128], BF16))
        cosb = e(nc.sbuf_tensor("cosb", [128, 8 * HD], F32))
        sinb = e(nc.sbuf_tensor("sinb", [128, 8 * HD], F32))
        r2t = e(nc.sbuf_tensor("r2t", [128, 512], BF16))
        tn = e(nc.sbuf_tensor("tn", [128, 512], F32))
        xf = e(nc.sbuf_tensor("xf", [128, 512], F32))
        rowa = e(nc.sbuf_tensor("rowa", [1, 512], F32))
        rowb = e(nc.sbuf_tensor("rowb", [1, 512], F32))
        rowc = e(nc.sbuf_tensor("rowc", [1, 512], F32))
        lgt = e(nc.sbuf_tensor("lgt", [128, 8], F32))
        srt = e(nc.sbuf_tensor("srt", [128, 8], F32))
        cum = e(nc.sbuf_tensor("cum", [128, 8], F32))
        tmp8 = e(nc.sbuf_tensor("tmp8", [128, 8], F32))
        w8 = e(nc.sbuf_tensor("w8", [128, 8], F32))
        sc1 = e(nc.sbuf_tensor("sc1", [128, 4], F32))
        hwb = e(nc.sbuf_tensor("hwb", [128, 128], BF16))
        hwT = e(nc.sbuf_tensor("hwT", [128, 512], BF16))
        pq = [e(nc.sbuf_tensor(f"pq{i}", [128, 512], BF16)) for i in range(2)]
        t1 = e(nc.sbuf_tensor("t1", [128, 512], F32))
        t2 = e(nc.sbuf_tensor("t2", [128, 512], F32))

        psA = [e(nc.psum_tensor(f"psA{i}", [128, 512], F32)) for i in range(2)]
        psS1 = e(nc.psum_tensor("psS1", [128, 512], F32))
        psS2 = e(nc.psum_tensor("psS2", [128, 512], F32))
        psRt = e(nc.psum_tensor("psRt", [128, 512], F32))
        psH = e(nc.psum_tensor("psH", [128, 512], F32))
        psT = e(nc.psum_tensor("psT", [128, 1024], BF16))

        P = Plan(nc)
        for sname in ("ld", "ldg", "lwa0", "lwa1", "lwb0", "lwb1", "lx0a",
                      "lx0b", "lot", "pe", "ac", "dv", "st"):
            P.sem(ctx, sname)

        P.add("sync", lambda E: E.dma_start(out=io8[:], in_=iota8), inc="ld", inck=16)
        P.add("sync", lambda E: E.dma_start(out=on8[:], in_=ones8), inc="ld", inck=16)
        P.add("sync", lambda E: E.dma_start(out=on1[:], in_=onesb), inc="ld", inck=16)
        P.add("sync", lambda E: E.dma_start(out=onr[:], in_=onesr), inc="ld", inck=16)
        P.add("sync", lambda E: E.dma_start(out=onf[:], in_=onesf), inc="ld", inck=16)
        P.add("sync", lambda E: E.dma_start(out=epsb[:], in_=epsr), inc="ld", inck=16)
        P.add("sync", lambda E: E.dma_start(out=idb[:], in_=idn), inc="ld", inck=16)
        P.add("sync", lambda E: E.dma_start(out=bbt[:], in_=bbr), inc="ld", inck=16)
        P.add("sync", lambda E: E.dma_start(
            out=cosb[:].rearrange("p (tc f) -> p tc f", f=HD),
            in_=cos1.rearrange("(tc p) f -> p tc f", p=128)), inc="ld", inck=16)
        P.add("sync", lambda E: E.dma_start(
            out=sinb[:].rearrange("p (tc f) -> p tc f", f=HD),
            in_=sin1.rearrange("(tc p) f -> p tc f", p=128)), inc="ld", inck=16)
        P.add("gpsimd", lambda E: E.dma_start(out=bqb[:], in_=Bq), inc="ldg", inck=16)
        P.add("gpsimd", lambda E: E.dma_start(
            out=wrb[:].rearrange("p (kc n) -> p kc n", n=8),
            in_=Wr.rearrange("(kc p) n -> p kc n", p=128)), inc="ldg", inck=16)
        P.add("gpsimd", lambda E: E.dma_start(out=brb[:], in_=brr), inc="ldg", inck=16)
        LD_ALL = P.cnt["ld"]
        LDG_ALL = P.cnt["ldg"]

        def stream_wtile(src, col0, ncols, reuse_pe=None):
            reuse = P.cnt["pe"] if reuse_pe is None else reuse_pe
            par = stream_wtile.i % 2
            sem = f"lwa{par}"

            def ldw(E, src=src, col0=col0, ncols=ncols, par=par):
                return E.dma_start(
                    out=wtile[par][:, 0:32 * ncols],
                    in_=src[:, col0 * 32:(col0 + ncols) * 32])
            P.add("gpsimd", ldw, waits=[("pe", reuse)], inc=sem, inck=16)
            stream_wtile.i += 1
            return (sem, P.cnt[sem]), par
        stream_wtile.i = 0

        def stream_wbig(src, col0, ncols, reuse_pe=None):
            reuse = P.cnt["pe"] if reuse_pe is None else reuse_pe
            par = stream_wbig.i % 2
            sem = f"lwb{par}"

            def ldw(E, src=src, col0=col0, ncols=ncols, par=par):
                return E.dma_start(
                    out=wbig[par][:, 0:32 * ncols],
                    in_=src[:, col0 * 32:(col0 + ncols) * 32])
            P.add("gpsimd", ldw, waits=[("pe", reuse)], inc=sem, inck=16)
            stream_wbig.i += 1
            return (sem, P.cnt[sem]), par
        stream_wbig.i = 0

        def ln_rows():
            P.add("vector", lambda E: E.tensor_scalar(
                out=rowa[:, 0:NT], in0=psS1[0:1, 0:NT], scalar1=1.0 / D,
                scalar2=None, op0=ALU.mult), waits=[("pe", P.cnt["pe"])])
            mu_dv = P.cnt["dv"]
            P.add("vector", lambda E: E.tensor_scalar(
                out=rowb[:, 0:NT], in0=psS2[0:1, 0:NT], scalar1=1.0 / D,
                scalar2=None, op0=ALU.mult))
            P.add("scalar", lambda E: E.activation(
                out=rowc[:, 0:NT], in_=rowa[:, 0:NT], func=AF.Square),
                waits=[("dv", mu_dv)])
            sq_ac = P.cnt["ac"]
            P.add("vector", lambda E: E.tensor_tensor(
                out=rowb[:, 0:NT], in0=rowb[:, 0:NT], in1=rowc[:, 0:NT],
                op=ALU.subtract), waits=[("ac", sq_ac)])
            var_dv = P.cnt["dv"]
            P.add("scalar", lambda E: E.activation(
                out=rowc[:, 0:NT], in_=rowb[:, 0:NT], func=AF.Ln,
                bias=epsb[0:1, 0:1], scale=1.0), waits=[("dv", var_dv)])
            P.add("scalar", lambda E: E.activation(
                out=rowc[:, 0:NT], in_=rowc[:, 0:NT], func=AF.Exp,
                bias=0.0, scale=-0.5))
            return P.cnt["ac"]

        def bcast_rows(rstd_ac):
            P.add("tensor", lambda E: E.matmul(
                psA[0][:, 0:NT], onf[0:1, 0:128], rowa[0:1, 0:NT],
                start=True, stop=True), waits=[("ac", rstd_ac)])
            P.add("tensor", lambda E: E.matmul(
                psA[1][:, 0:NT], onf[0:1, 0:128], rowc[0:1, 0:NT],
                start=True, stop=True))
            return P.cnt["pe"]

        def matvec_block(wpar, rhs_ap_fn, doc, extra=(), close=True):
            for kc in range(32):
                def mmw(E, kc=kc, doc=doc, wpar=wpar, close=close):
                    return E.matmul(
                        psA[doc % 2][:], wtile[wpar][:, kc * 128:(kc + 1) * 128],
                        rhs_ap_fn(kc), start=(kc == 0),
                        stop=(close and kc == 31))
                P.add("tensor", mmw, waits=(list(extra) if kc == 0 else []))
            return P.cnt["pe"]

        def ln_block(src_buf, dst_buf, resid_buf, store_f32, tc):
            rstd_ac = ln_rows()
            bc_pe = bcast_rows(rstd_ac)
            for doc in range(32):
                def nrm1(E, doc=doc):
                    return E.tensor_tensor(
                        out=tn[:], in0=src_buf[:, doc * NT:(doc + 1) * NT],
                        in1=psA[0][:], op=ALU.subtract)
                P.add("vector", nrm1, waits=([("pe", bc_pe)] if doc == 0 else []))

                def nrm2(E, doc=doc):
                    return E.tensor_tensor(
                        out=dst_buf[:, doc * NT:(doc + 1) * NT], in0=tn[:],
                        in1=psA[1][:], op=ALU.mult)
                P.add("vector", nrm2)
                n2_dv = P.cnt["dv"]
                if store_f32 is not None:
                    def cvt(E, doc=doc):
                        return E.activation(
                            out=xf[:], in_=dst_buf[:, doc * NT:(doc + 1) * NT],
                            func=AF.Copy)
                    P.add("scalar", cvt, waits=[("dv", n2_dv), ("st", P.cnt["st"])])
                    cvt_ac = P.cnt["ac"]

                    def stx(E, doc=doc, tc=tc):
                        return E.dma_start(
                            out=store_f32[doc * 128:(doc + 1) * 128,
                                          tc * NT:(tc + 1) * NT], in_=xf[:])
                    P.add("scalar", stx, waits=[("ac", cvt_ac)], inc="st", inck=16)
            return P.cnt["dv"]

        x0_reuse = []
        for tc in range(2):
            tc0 = tc * NT
            def ldot(E, tc0=tc0):
                return E.dma_start(
                    out=wbig[0][:].rearrange("p (kc n) -> p kc n", n=512),
                    in_=oTm[:, tc0:tc0 + NT].rearrange("(kc p) n -> p kc n", p=128))
            P.add("sync", ldot, waits=[("pe", P.cnt["pe"])], inc="lot", inck=16)
            ot_ld = P.cnt["lot"]

            wt_cons = [P.cnt["pe"], P.cnt["pe"]]
            for doc in range(32):
                wld, wpar = stream_wtile(wo0, doc * 128, 128,
                                         reuse_pe=wt_cons[doc % 2])

                def ldx0(E, doc=doc, tc0=tc0, par=doc % 2):
                    return E.dma_start(
                        out=x0c[par][:],
                        in_=x0T[doc * 128:(doc + 1) * 128, tc0:tc0 + NT])
                x0sem = "lx0a" if doc % 2 == 0 else "lx0b"
                P.add("sync", ldx0,
                      waits=([("dv", x0_reuse[-2])] if len(x0_reuse) >= 2 else []),
                      inc=x0sem, inck=16)
                x0_ld = (x0sem, P.cnt[x0sem])

                wl = [wld, ("lot", ot_ld)]
                if doc == 0 and tc == 0:
                    wl.append(("ld", LD_ALL))
                    wl.append(("ldg", LDG_ALL))
                if doc >= 2:
                    wl.append(("dv", x0_reuse[-2]))
                else:
                    wl.append(("dv", P.cnt["dv"]))
                mm_pe = matvec_block(
                    wpar, lambda kc: wbig[0][:, kc * 512:(kc + 1) * 512], doc,
                    extra=wl)
                wt_cons[doc % 2] = mm_pe

                def evr(E, doc=doc, par=doc % 2):
                    return E.tensor_tensor(
                        out=rbuf[:, doc * NT:(doc + 1) * NT],
                        in0=psA[doc % 2][:], in1=x0c[par][:], op=ALU.add)
                P.add("vector", evr, waits=[("pe", mm_pe), x0_ld])
                x0_reuse.append(P.cnt["dv"])
                ev_dv = P.cnt["dv"]

                def sqr(E, doc=doc):
                    return E.activation(
                        out=r2t[:], in_=rbuf[:, doc * NT:(doc + 1) * NT],
                        func=AF.Square)
                P.add("scalar", sqr,
                      waits=[("dv", ev_dv), ("pe", P.cnt["pe"])])
                sq_ac = P.cnt["ac"]

                def st1(E, doc=doc):
                    return E.matmul(psS1[0:1, 0:NT], on1[:],
                                    rbuf[:, doc * NT:(doc + 1) * NT],
                                    start=(doc == 0), stop=(doc == 31))
                P.add("tensor", st1, waits=[("dv", ev_dv)])

                def st2(E, doc=doc):
                    return E.matmul(psS2[0:1, 0:NT], on1[:], r2t[:],
                                    start=(doc == 0), stop=(doc == 31))
                P.add("tensor", st2, waits=[("ac", sq_ac)])
            ln_block(rbuf, x1buf, None, None, tc)
            x1_dv = P.cnt["dv"]

            if phases < 2:
                continue
            def ldaq(E):
                return E.dma_start(
                    out=wbig[1][:, 0:32 * 128].rearrange("p (kc n) -> p kc n", n=128),
                    in_=Aq.rearrange("(kc p) n -> p kc n", p=128))
            P.add("gpsimd", ldaq, waits=[("pe", P.cnt["pe"])], inc="lwb1", inck=16)
            aq_ld = ("lwb1", P.cnt["lwb1"])
            for tt in range(4):
                for kc in range(32):
                    def mmr(E, kc=kc, tt=tt):
                        return E.matmul(
                            psRt[:, 0:8],
                            x1buf[:, kc * NT + tt * 128:kc * NT + (tt + 1) * 128],
                            wrb[:, kc * 8:(kc + 1) * 8],
                            start=(kc == 0), stop=False)
                    wl_r = []
                    if tt == 0 and kc == 0:
                        wl_r = [("dv", x1_dv)]
                    elif kc == 0:
                        wl_r = [("ac", P.cnt["ac"])]
                    P.add("tensor", mmr, waits=wl_r)

                def mmrb(E):
                    return E.matmul(psRt[:, 0:8], onr[:], brb[:],
                                    start=False, stop=True)
                P.add("tensor", mmrb)
                rt_pe = P.cnt["pe"]

                def evl(E):
                    return E.activation(out=lgt[:], in_=psRt[:, 0:8], func=AF.Copy)
                P.add("scalar", evl, waits=[("pe", rt_pe)])
                lgt_ac = P.cnt["ac"]

                P.add("vector", lambda E: E.max(out=srt[:], in_=lgt[:]),
                      waits=[("ac", lgt_ac)])
                P.add("vector", lambda E: E.tensor_tensor_scan(
                    out=cum[:], data0=on8[:], data1=srt[:], initial=0.0,
                    op0=ALU.mult, op1=ALU.add))
                P.add("vector", lambda E: E.tensor_tensor(
                    out=tmp8[:], in0=srt[:], in1=io8[:], op=ALU.mult))
                P.add("vector", lambda E: E.tensor_scalar(
                    out=tmp8[:], in0=tmp8[:], scalar1=1.0, scalar2=None,
                    op0=ALU.add))
                P.add("vector", lambda E: E.tensor_tensor(
                    out=tmp8[:], in0=tmp8[:], in1=cum[:], op=ALU.is_gt))
                P.add("vector", lambda E: E.tensor_scalar(
                    out=w8[:], in0=tmp8[:], scalar1=0.0, scalar2=None,
                    op0=ALU.add, op1=ALU.add, accum_out=sc1[:, 0:1]))
                P.add("vector", lambda E: E.tensor_tensor(
                    out=cum[:], in0=srt[:], in1=tmp8[:], op=ALU.mult))
                P.add("vector", lambda E: E.tensor_scalar(
                    out=w8[:], in0=cum[:], scalar1=0.0, scalar2=None,
                    op0=ALU.add, op1=ALU.add, accum_out=sc1[:, 1:2]))
                P.add("vector", lambda E: E.reciprocal(
                    out=sc1[:, 2:3], in_=sc1[:, 0:1]))
                P.add("vector", lambda E: E.tensor_scalar(
                    out=sc1[:, 3:4], in0=sc1[:, 1:2], scalar1=1.0, scalar2=None,
                    op0=ALU.subtract))
                P.add("vector", lambda E: E.tensor_tensor(
                    out=sc1[:, 3:4], in0=sc1[:, 3:4], in1=sc1[:, 2:3],
                    op=ALU.mult))
                P.add("vector", lambda E: E.tensor_scalar(
                    out=w8[:], in0=lgt[:], scalar1=sc1[:, 3:4], scalar2=0.0,
                    op0=ALU.subtract, op1=ALU.max))
                w8_dv = P.cnt["dv"]

                for kc in range(32):
                    def mmh(E, kc=kc, tt=tt):
                        return E.matmul(
                            psH[:, 0:128],
                            x1buf[:, kc * NT + tt * 128:kc * NT + (tt + 1) * 128],
                            wbig[1][:, kc * 128:(kc + 1) * 128],
                            start=(kc == 0), stop=(kc == 31))
                    P.add("tensor", mmh,
                          waits=([aq_ld] if kc == 0 else []))
                h_pe = P.cnt["pe"]

                def hwmul(E):
                    return E.tensor_tensor(
                        out=hwb[:].rearrange("p (e r) -> p e r", r=16),
                        in0=psH[:, 0:128].rearrange("p (e r) -> p e r", r=16),
                        in1=w8[:].unsqueeze(2).to_broadcast([128, 8, 16]),
                        op=ALU.mult)
                P.add("vector", hwmul, waits=[("pe", h_pe), ("dv", w8_dv)])
                hw_dv = P.cnt["dv"]

                def htr(E):
                    return E.transpose(psT[:, 0:128], hwb[:], idb[:])
                P.add("tensor", htr, waits=[("dv", hw_dv), ("ac", P.cnt["ac"])])
                htr_pe = P.cnt["pe"]

                def hev(E, tt=tt):
                    return E.activation(out=hwT[:, tt * 128:(tt + 1) * 128],
                                        in_=psT[:, 0:128], func=AF.Copy)
                P.add("scalar", hev, waits=[("pe", htr_pe)])
            hwT_ac = P.cnt["ac"]

            ev2_list = []
            wt_cons2 = [P.cnt["pe"], P.cnt["pe"]]
            for doc in range(32):
                wld, wpar = stream_wtile(Wb, doc * 128, 128,
                                         reuse_pe=wt_cons2[doc % 2])
                wl = [wld]
                if doc == 0:
                    wl.append(("ac", hwT_ac))
                if doc >= 2:
                    wl.append(("dv", ev2_list[-2]))
                else:
                    wl.append(("dv", P.cnt["dv"]))
                mm_pe = matvec_block(
                    wpar, lambda kc: x1buf[:, kc * NT:(kc + 1) * NT], doc,
                    extra=wl, close=False)

                def mmexp(E, doc=doc):
                    return E.matmul(
                        psA[doc % 2][:], bqb[:, doc * 128:(doc + 1) * 128],
                        hwT[:], start=False, stop=True)
                P.add("tensor", mmexp)
                mm_pe = P.cnt["pe"]
                wt_cons2[doc % 2] = mm_pe

                def evr2(E, doc=doc):
                    return E.scalar_tensor_tensor(
                        out=rbuf[:, doc * NT:(doc + 1) * NT],
                        in0=psA[doc % 2][:], scalar=bbt[:, doc:doc + 1],
                        in1=x1buf[:, doc * NT:(doc + 1) * NT],
                        op0=ALU.add, op1=ALU.add)
                P.add("vector", evr2, waits=[("pe", mm_pe)])
                ev_dv = P.cnt["dv"]
                ev2_list.append(ev_dv)

                def sqr2(E, doc=doc):
                    return E.activation(
                        out=r2t[:], in_=rbuf[:, doc * NT:(doc + 1) * NT],
                        func=AF.Square)
                P.add("scalar", sqr2,
                      waits=[("dv", ev_dv), ("pe", P.cnt["pe"])])
                sq_ac = P.cnt["ac"]

                def st1b(E, doc=doc):
                    return E.matmul(psS1[0:1, 0:NT], on1[:],
                                    rbuf[:, doc * NT:(doc + 1) * NT],
                                    start=(doc == 0), stop=(doc == 31))
                P.add("tensor", st1b, waits=[("dv", ev_dv)])

                def st2b(E, doc=doc):
                    return E.matmul(psS2[0:1, 0:NT], on1[:], r2t[:],
                                    start=(doc == 0), stop=(doc == 31))
                P.add("tensor", st2b, waits=[("ac", sq_ac)])
            ln_block(rbuf, rbuf, None, x2To, tc)
            x2_dv = P.cnt["dv"]

            if phases < 3:
                continue
            wsrc1 = [(wq1, dc * 512, 0) for dc in range(8)] + \
                    [(wk1, dc * 512, 1) for dc in range(2)] + \
                    [(wv1, dc * 512, 2) for dc in range(2)]
            out1 = [(q1o, dc * 512) for dc in range(8)] + \
                   [(k1o, dc * 512) for dc in range(2)] + \
                   [(v1o, dc * 512) for dc in range(2)]
            wb_cons = [P.cnt["pe"], P.cnt["pe"]]
            for i, (wsrc, c0, kind) in enumerate(wsrc1):
                wld, wpar = stream_wbig(wsrc, c0, 512,
                                        reuse_pe=wb_cons[stream_wbig.i % 2])
                for tt in range(4):
                    gidx = i * 4 + tt
                    wl = [wld]
                    if i == 0 and tt == 0:
                        wl.append(("dv", x2_dv))
                    else:
                        wl.append(("dv", P.cnt["dv"]))
                    for kc in range(32):
                        def mmq(E, kc=kc, tt=tt, wpar=wpar, gidx=gidx):
                            return E.matmul(
                                psA[gidx % 2][:],
                                rbuf[:, kc * NT + tt * 128:kc * NT + (tt + 1) * 128],
                                wbig[wpar][:, kc * 512:(kc + 1) * 512],
                                start=(kc == 0), stop=(kc == 31))
                        P.add("tensor", mmq, waits=(wl if kc == 0 else []))
                    mm_pe = P.cnt["pe"]
                    wb_cons[wpar] = mm_pe

                    par = gidx % 2
                    tglob = tc * 4 + tt
                    if kind in (0, 1):
                        pv = psA[par][:].rearrange("p (h f) -> p h f", f=HD)
                        ov = pq[par][:].rearrange("p (h f) -> p h f", f=HD)
                        c3 = cosb[:, tglob * HD:(tglob + 1) * HD].unsqueeze(1) \
                            .to_broadcast([128, 4, HD])
                        s3 = sinb[:, tglob * HD:(tglob + 1) * HD].unsqueeze(1) \
                            .to_broadcast([128, 4, HD])
                        t1v = t1[:].rearrange("p (h f) -> p h f", f=HD)
                        t2v = t2[:, 0:256].rearrange("p (h f) -> p h f", f=64)

                        def r1(E, pv=pv, c3=c3, t1v=t1v):
                            return E.tensor_tensor(out=t1v, in0=pv, in1=c3,
                                                   op=ALU.mult)
                        P.add("vector", r1,
                              waits=[("pe", mm_pe), ("st", P.cnt["st"])])

                        def r2(E, pv=pv, s3=s3, t2v=t2v):
                            return E.tensor_tensor(
                                out=t2v, in0=pv[:, :, 64:128],
                                in1=s3[:, :, 0:64], op=ALU.mult)
                        P.add("vector", r2)

                        def r3(E, ov=ov, t1v=t1v, t2v=t2v):
                            return E.tensor_tensor(
                                out=ov[:, :, 0:64], in0=t1v[:, :, 0:64],
                                in1=t2v, op=ALU.subtract)
                        P.add("vector", r3)

                        def r4(E, pv=pv, s3=s3, t2v=t2v):
                            return E.tensor_tensor(
                                out=t2v, in0=pv[:, :, 0:64],
                                in1=s3[:, :, 64:128], op=ALU.mult)
                        P.add("vector", r4)

                        def r5(E, ov=ov, t1v=t1v, t2v=t2v):
                            return E.tensor_tensor(
                                out=ov[:, :, 64:128], in0=t1v[:, :, 64:128],
                                in1=t2v, op=ALU.add)
                        P.add("vector", r5)
                        rp_dv = P.cnt["dv"]
                        stw = [("dv", rp_dv)]
                    else:
                        def vcp(E, par=par):
                            return E.tensor_copy(out=pq[par][:], in_=psA[par][:])
                        P.add("vector", vcp,
                              waits=[("pe", mm_pe), ("st", P.cnt["st"])])
                        stw = [("dv", P.cnt["dv"])]

                    dst, dc0 = out1[i]

                    def stq(E, dst=dst, dc0=dc0, tt=tt, tc=tc, par=par):
                        return E.dma_start(
                            out=dst[tc * NT + tt * 128:tc * NT + (tt + 1) * 128,
                                    dc0:dc0 + 512], in_=pq[par][:])
                    P.add("scalar", stq, waits=stw, inc="st", inck=16)

        P.ops["scalar"].append(
            (lambda E: E.wait_ge(P.sems["st"], P.cnt["st"]), [], None, 1))
        P.check()
        block = e(nc.Block())
        P.emit(block)
    return nc


def hosttile(w, cc):
    K, N = w.shape
    return np.ascontiguousarray(
        w.reshape(K // 128, 128, N // cc, cc).transpose(1, 2, 0, 3)
        .reshape(128, (N // cc) * (K // 128) * cc))


def launch_C_inputs(inputs, oT_glob):
    x0 = np.asarray(inputs["L0_attn_x"], np.float32).reshape(B * S, D)
    cos1, sin1 = rope1_tables()
    wo0 = hosttile(np.asarray(inputs["L0_attn_wo"], np.float32), 128)
    Wbt = hosttile(np.ascontiguousarray(
        np.asarray(inputs["L0_ffn_W_base"], np.float32).T), 128)
    bb = np.asarray(inputs["L0_ffn_b_base"], np.float32)
    bbt = np.ascontiguousarray(bb.reshape(32, 128).T)
    Wrt = np.ascontiguousarray(np.asarray(inputs["L0_ffn_W_router"], np.float32).T)
    br = np.asarray(inputs["L0_ffn_b_router"], np.float32).reshape(1, 8)
    Aq = np.ascontiguousarray(np.asarray(inputs["L0_ffn_lora_A"], np.float32)
                              .transpose(1, 0, 2).reshape(D, 128))
    Bq = np.ascontiguousarray(np.asarray(inputs["L0_ffn_lora_B"], np.float32)
                              .reshape(128, D))
    wq1t = hosttile(np.ascontiguousarray(
        np.asarray(inputs["L1_attn_Wq"], np.float32).T), 512)
    wk1t = hosttile(np.ascontiguousarray(
        np.asarray(inputs["L1_attn_Wk"], np.float32).T), 512)
    wv1t = hosttile(np.ascontiguousarray(
        np.asarray(inputs["L1_attn_Wv"], np.float32).T), 512)
    iota = np.tile(np.arange(1, 9, dtype=np.float32)[None, :], (128, 1))
    ones8 = np.ones((128, 8), np.float32)
    onesb = np.ones((128, 1), BF)
    onesr = np.ones((1, 128), BF)
    onesf = np.ones((1, 512), np.float32)
    epsr = np.full((1, 1), 1e-5, np.float32)
    idn = np.eye(128, dtype=BF)
    maps = []
    for c in range(NCORES):
        t0 = c * TOK
        tok0 = (c % 4) * TOK
        m = dict(
            x0T=np.ascontiguousarray(x0[t0:t0 + TOK].T),
            oTm=np.ascontiguousarray(np.asarray(oT_glob)[:, t0:t0 + TOK]),
            wo0=wo0, Wb=Wbt, bbr=bbt, Wr=Wrt, brr=br, Aq=Aq, Bq=Bq,
            wq1=wq1t, wk1=wk1t, wv1=wv1t,
            cos1=np.ascontiguousarray(cos1[tok0:tok0 + TOK]),
            sin1=np.ascontiguousarray(sin1[tok0:tok0 + TOK]),
            iota8=iota, ones8=ones8, onesb=onesb, onesr=onesr, onesf=onesf,
            epsr=epsr, idn=idn,
        )
        maps.append(m)
    return maps



def build_launch_E(phases=2):
    nc = bass.Bass("TRN2", target_bir_lowering=False, debug=False)
    x2T = nc.dram_tensor("x2T", [D, TOK], F32, kind="ExternalInput").ap()
    oTm = nc.dram_tensor("oTm", [D, TOK], BF16, kind="ExternalInput").ap()
    wo1 = nc.dram_tensor("wo1", [128, D * D // 128], F32, kind="ExternalInput").ap()
    wgt = nc.dram_tensor("wgt", [128, 86 * 32 * 128], BF16, kind="ExternalInput").ap()
    wut = nc.dram_tensor("wut", [128, 86 * 32 * 128], BF16, kind="ExternalInput").ap()
    wdt = nc.dram_tensor("wdt", [128, 32 * 86 * 128], BF16, kind="ExternalInput").ap()
    onesb = nc.dram_tensor("onesb", [128, 1], BF16, kind="ExternalInput").ap()
    onesf = nc.dram_tensor("onesf", [1, 512], F32, kind="ExternalInput").ap()
    epsr = nc.dram_tensor("epsr", [1, 1], F32, kind="ExternalInput").ap()
    xout = nc.dram_tensor("xout", [D, TOK], F32, kind="ExternalOutput").ap()
    h_scr = nc.dram_tensor("h_scr", [86 * 128, TOK], BF16, kind="Internal").ap()

    NT = 512
    NIC = 86
    DGRPS = [(0, 6), (6, 6), (12, 6), (18, 6), (24, 6), (30, 2)]

    with ExitStack() as ctx:
        e = ctx.enter_context
        x3buf = e(nc.sbuf_tensor("x3buf", [128, 32 * NT], BF16))
        rbuf = e(nc.sbuf_tensor("rbuf", [128, 32 * NT], BF16))
        obuf = e(nc.sbuf_tensor("obuf", [128, 32 * NT], BF16))
        wt4 = [e(nc.sbuf_tensor(f"wt4_{i}", [128, 32 * 128], BF16)) for i in range(4)]
        wdbuf = [e(nc.sbuf_tensor(f"wdbuf{i}", [128, 6 * 128], BF16)) for i in range(2)]
        hbuf = [e(nc.sbuf_tensor(f"hbuf{i}", [128, 512], BF16)) for i in range(2)]
        x2c = [e(nc.sbuf_tensor(f"x2c{i}", [128, 512], F32)) for i in range(2)]
        gsb = e(nc.sbuf_tensor("gsb", [128, 512], BF16))
        htile = e(nc.sbuf_tensor("htile", [128, 512], BF16))
        r2t = e(nc.sbuf_tensor("r2t", [128, 512], BF16))
        tn = e(nc.sbuf_tensor("tn", [128, 512], F32))
        xf = e(nc.sbuf_tensor("xf", [128, 512], F32))
        rowa = e(nc.sbuf_tensor("rowa", [1, 512], F32))
        rowb = e(nc.sbuf_tensor("rowb", [1, 512], F32))
        rowc = e(nc.sbuf_tensor("rowc", [1, 512], F32))
        on1 = e(nc.sbuf_tensor("on1", [128, 1], BF16))
        onf = e(nc.sbuf_tensor("onf", [1, 512], F32))
        epsb = e(nc.sbuf_tensor("epsb", [1, 1], F32))

        psD = [e(nc.psum_tensor(f"psD{i}", [128, 512], F32)) for i in range(6)]
        psS1 = e(nc.psum_tensor("psS1", [128, 512], F32))
        psS2 = e(nc.psum_tensor("psS2", [128, 512], F32))

        P = Plan(nc)
        for sname in ("ld", "lwa0", "lwa1", "lwg0", "lwg1", "lwu0", "lwu1",
                      "lwd0", "lwd1", "lh0", "lh1", "lx0a", "lx0b", "lot",
                      "pe", "ac", "dv", "st", "sth"):
            P.sem(ctx, sname)

        P.add("sync", lambda E: E.dma_start(out=on1[:], in_=onesb), inc="ld", inck=16)
        P.add("sync", lambda E: E.dma_start(out=onf[:], in_=onesf), inc="ld", inck=16)
        P.add("sync", lambda E: E.dma_start(out=epsb[:], in_=epsr), inc="ld", inck=16)
        LD_ALL = P.cnt["ld"]

        def ln_rows():
            P.add("vector", lambda E: E.tensor_scalar(
                out=rowa[:, 0:NT], in0=psS1[0:1, 0:NT], scalar1=1.0 / D,
                scalar2=None, op0=ALU.mult), waits=[("pe", P.cnt["pe"])])
            mu_dv = P.cnt["dv"]
            P.add("vector", lambda E: E.tensor_scalar(
                out=rowb[:, 0:NT], in0=psS2[0:1, 0:NT], scalar1=1.0 / D,
                scalar2=None, op0=ALU.mult))
            P.add("scalar", lambda E: E.activation(
                out=rowc[:, 0:NT], in_=rowa[:, 0:NT], func=AF.Square),
                waits=[("dv", mu_dv)])
            sq_ac = P.cnt["ac"]
            P.add("vector", lambda E: E.tensor_tensor(
                out=rowb[:, 0:NT], in0=rowb[:, 0:NT], in1=rowc[:, 0:NT],
                op=ALU.subtract), waits=[("ac", sq_ac)])
            var_dv = P.cnt["dv"]
            P.add("scalar", lambda E: E.activation(
                out=rowc[:, 0:NT], in_=rowb[:, 0:NT], func=AF.Ln,
                bias=epsb[0:1, 0:1], scale=1.0), waits=[("dv", var_dv)])
            P.add("scalar", lambda E: E.activation(
                out=rowc[:, 0:NT], in_=rowc[:, 0:NT], func=AF.Exp,
                bias=0.0, scale=-0.5))
            return P.cnt["ac"]

        def bcast_rows(rstd_ac):
            P.add("tensor", lambda E: E.matmul(
                psD[0][:, 0:NT], onf[0:1, 0:128], rowa[0:1, 0:NT],
                start=True, stop=True), waits=[("ac", rstd_ac)])
            P.add("tensor", lambda E: E.matmul(
                psD[1][:, 0:NT], onf[0:1, 0:128], rowc[0:1, 0:NT],
                start=True, stop=True))
            return P.cnt["pe"]

        wt_i = [0]

        def stream_wt(src, row0, nrows, ncols, pool):
            reuse = P.cnt["pe"]
            par = wt_i[0] % 2
            sem = f"{pool}{par}"
            if pool == "lwa":
                buf = wt4[par]
            elif pool == "lwg":
                buf = wt4[par]
            elif pool == "lwu":
                buf = wt4[2 + par]
            else:
                buf = wdbuf[par]

            def ldw(E, src=src, row0=row0, nrows=nrows, ncols=ncols, buf=buf):
                return E.dma_start(
                    out=buf[:, 0:nrows * ncols].rearrange(
                        "p (kc n) -> p kc n", n=ncols),
                    in_=src[row0:row0 + nrows * 128, 0:ncols].rearrange(
                        "(kc p) n -> p kc n", p=128))
            P.add("sync", ldw, waits=[("pe", reuse)], inc=sem, inck=16)
            return (sem, P.cnt[sem]), (par if pool != "lwu" else par)

        x2_reuse = []
        for tc in range(2):
            tc0 = tc * NT
            def ldot(E, tc0=tc0):
                return E.dma_start(
                    out=obuf[:].rearrange("p (kc n) -> p kc n", n=512),
                    in_=oTm[:, tc0:tc0 + NT].rearrange("(kc p) n -> p kc n", p=128))
            P.add("gpsimd", ldot, waits=[("pe", P.cnt["pe"])], inc="lot", inck=16)
            ot_ld = P.cnt["lot"]

            wo_cons = [P.cnt["pe"], P.cnt["pe"]]
            for doc in range(32):
                reuse = wo_cons[doc % 2]
                par = doc % 2
                wsem = f"lwa{par}"

                def ldw1(E, doc=doc, par=par):
                    return E.dma_start(
                        out=wt4[par][:],
                        in_=wo1[:, doc * 4096:(doc + 1) * 4096])
                P.add("gpsimd", ldw1, waits=[("pe", reuse)], inc=wsem, inck=16)
                wld = (wsem, P.cnt[wsem])

                def ldx2(E, doc=doc, tc0=tc0, par=par):
                    return E.dma_start(
                        out=x2c[par][:],
                        in_=x2T[doc * 128:(doc + 1) * 128, tc0:tc0 + NT])
                x2sem = "lx0a" if par == 0 else "lx0b"
                P.add("sync", ldx2,
                      waits=([("dv", x2_reuse[-2])] if len(x2_reuse) >= 2 else []),
                      inc=x2sem, inck=16)
                x2_ld = (x2sem, P.cnt[x2sem])

                wl = [wld, ("lot", ot_ld)]
                if doc == 0 and tc == 0:
                    wl.append(("ld", LD_ALL))
                if doc >= 2:
                    wl.append(("dv", x2_reuse[-2]))
                else:
                    wl.append(("dv", P.cnt["dv"]))
                    wl.append(("ac", P.cnt["ac"]))
                for kc in range(32):
                    def mmwo(E, kc=kc, doc=doc, par=par):
                        return E.matmul(
                            psD[doc % 2][:], wt4[par][:, kc * 128:(kc + 1) * 128],
                            obuf[:, kc * 512:(kc + 1) * 512],
                            start=(kc == 0), stop=(kc == 31))
                    P.add("tensor", mmwo, waits=(wl if kc == 0 else []))
                mm_pe = P.cnt["pe"]
                wo_cons[doc % 2] = mm_pe

                def evr(E, doc=doc, par=par):
                    return E.tensor_tensor(
                        out=rbuf[:, doc * NT:(doc + 1) * NT],
                        in0=psD[doc % 2][:], in1=x2c[par][:], op=ALU.add)
                P.add("vector", evr, waits=[("pe", mm_pe), x2_ld])
                x2_reuse.append(P.cnt["dv"])
                ev_dv = P.cnt["dv"]

                def sqr(E, doc=doc):
                    return E.activation(
                        out=r2t[:], in_=rbuf[:, doc * NT:(doc + 1) * NT],
                        func=AF.Square)
                P.add("scalar", sqr,
                      waits=[("dv", ev_dv), ("pe", P.cnt["pe"])])
                sq_ac = P.cnt["ac"]

                def st1(E, doc=doc):
                    return E.matmul(psS1[0:1, 0:NT], on1[:],
                                    rbuf[:, doc * NT:(doc + 1) * NT],
                                    start=(doc == 0), stop=(doc == 31))
                P.add("tensor", st1, waits=[("dv", ev_dv)])

                def st2(E, doc=doc):
                    return E.matmul(psS2[0:1, 0:NT], on1[:], r2t[:],
                                    start=(doc == 0), stop=(doc == 31))
                P.add("tensor", st2, waits=[("ac", sq_ac)])

            rstd_ac = ln_rows()
            bc_pe = bcast_rows(rstd_ac)
            for doc in range(32):
                def nrm1(E, doc=doc):
                    return E.tensor_tensor(
                        out=tn[:], in0=rbuf[:, doc * NT:(doc + 1) * NT],
                        in1=psD[0][:], op=ALU.subtract)
                P.add("vector", nrm1, waits=([("pe", bc_pe)] if doc == 0 else []))

                def nrm2(E, doc=doc):
                    return E.tensor_tensor(
                        out=x3buf[:, doc * NT:(doc + 1) * NT], in0=tn[:],
                        in1=psD[1][:], op=ALU.mult)
                P.add("vector", nrm2)
                if phases == 1:
                    n2_dv = P.cnt["dv"]

                    def cvt1(E, doc=doc):
                        return E.activation(
                            out=xf[:], in_=x3buf[:, doc * NT:(doc + 1) * NT],
                            func=AF.Copy)
                    P.add("scalar", cvt1, waits=[("dv", n2_dv),
                                                 ("st", P.cnt["st"])])
                    cv_ac = P.cnt["ac"]

                    def stx1(E, doc=doc, tc0=tc0):
                        return E.dma_start(
                            out=xout[doc * 128:(doc + 1) * 128, tc0:tc0 + NT],
                            in_=xf[:])
                    P.add("scalar", stx1, waits=[("ac", cv_ac)], inc="st", inck=16)
            x3_dv = P.cnt["dv"]
            if phases == 1:
                continue

            h_done = []
            si_list = []
            gu_cons = [P.cnt["pe"], P.cnt["pe"]]
            for ic in range(NIC):
                reuse = gu_cons[ic % 2]
                par = ic % 2

                def ldg(E, ic=ic, par=par):
                    return E.dma_start(
                        out=wt4[par][:],
                        in_=wgt[:, ic * 4096:(ic + 1) * 4096])
                P.add("sync", ldg, waits=[("pe", reuse)], inc=f"lwg{par}", inck=16)
                g_ld = (f"lwg{par}", P.cnt[f"lwg{par}"])

                def ldu(E, ic=ic, par=par):
                    return E.dma_start(
                        out=wt4[2 + par][:],
                        in_=wut[:, ic * 4096:(ic + 1) * 4096])
                P.add("sync", ldu, waits=[("pe", reuse)], inc=f"lwu{par}", inck=16)
                u_ld = (f"lwu{par}", P.cnt[f"lwu{par}"])

                wl = [g_ld]
                if ic == 0:
                    wl.append(("dv", x3_dv))
                    wl.append(("dv", P.cnt["dv"]))
                    wl.append(("ac", P.cnt["ac"]))
                else:
                    wl.append(("ac", si_list[-1]))
                for kc in range(32):
                    def mmg(E, kc=kc, par=par):
                        return E.matmul(
                            psD[0][:], wt4[par][:, kc * 128:(kc + 1) * 128],
                            x3buf[:, kc * NT:(kc + 1) * NT],
                            start=(kc == 0), stop=(kc == 31))
                    P.add("tensor", mmg, waits=(wl if kc == 0 else []))
                g_pe = P.cnt["pe"]
                for kc in range(32):
                    def mmu(E, kc=kc, par=par):
                        return E.matmul(
                            psD[1][:], wt4[2 + par][:, kc * 128:(kc + 1) * 128],
                            x3buf[:, kc * NT:(kc + 1) * NT],
                            start=(kc == 0), stop=(kc == 31))
                    wl_u = [u_ld]
                    if h_done:
                        wl_u.append(("dv", h_done[-1]))
                    else:
                        wl_u.append(("dv", P.cnt["dv"]))
                    P.add("tensor", mmu, waits=(wl_u if kc == 0 else []))
                u_pe = P.cnt["pe"]
                gu_cons[ic % 2] = u_pe

                def silu(E):
                    return E.activation(out=gsb[:], in_=psD[0][:], func=AF.Silu)
                P.add("scalar", silu, waits=[("pe", g_pe), ("sth", P.cnt["sth"])])
                si_ac = P.cnt["ac"]
                si_list.append(si_ac)

                def hmul(E):
                    return E.tensor_tensor(out=htile[:], in0=gsb[:],
                                           in1=psD[1][:], op=ALU.mult)
                P.add("vector", hmul, waits=[("pe", u_pe), ("ac", si_ac),
                                             ("sth", P.cnt["sth"])])
                h_dv = P.cnt["dv"]
                h_done.append(h_dv)

                def sth_(E, ic=ic, tc0=tc0):
                    return E.dma_start(
                        out=h_scr[ic * 128:(ic + 1) * 128, tc0:tc0 + NT],
                        in_=htile[:])
                P.add("scalar", sth_, waits=[("dv", h_dv)], inc="sth", inck=16)
            h_all = P.cnt["sth"]

            for gi, (d0, nd) in enumerate(DGRPS):
                grp_dv = P.cnt["dv"]
                grp_ac = P.cnt["ac"]
                dh_cons = [P.cnt["pe"], P.cnt["pe"]]
                for ic in range(NIC):
                    reuse = dh_cons[ic % 2]
                    par = ic % 2

                    def ldwd(E, ic=ic, d0=d0, nd=nd, par=par):
                        return E.dma_start(
                            out=wdbuf[par][:, 0:nd * 128],
                            in_=wdt[:, (ic * 32 + d0) * 128:
                                    (ic * 32 + d0 + nd) * 128])
                    P.add("sync", ldwd, waits=[("pe", reuse)],
                          inc=f"lwd{par}", inck=16)
                    wd_ld = (f"lwd{par}", P.cnt[f"lwd{par}"])

                    def ldh(E, ic=ic, tc0=tc0, par=par):
                        return E.dma_start(
                            out=hbuf[par][:],
                            in_=h_scr[ic * 128:(ic + 1) * 128, tc0:tc0 + NT])
                    P.add("sync", ldh, waits=[("sth", h_all), ("pe", reuse)],
                          inc=f"lh{par}", inck=16)
                    h_ld = (f"lh{par}", P.cnt[f"lh{par}"])

                    for d6 in range(nd):
                        def mmd(E, d6=d6, ic=ic, par=par):
                            return E.matmul(
                                psD[d6][:], wdbuf[par][:, d6 * 128:(d6 + 1) * 128],
                                hbuf[par][:],
                                start=(ic == 0), stop=(ic == NIC - 1))
                        wl_d = [wd_ld, h_ld]
                        if ic == 0:
                            wl_d += [("dv", grp_dv), ("ac", grp_ac)]
                        P.add("tensor", mmd,
                              waits=(wl_d if d6 == 0 else []))
                    dh_cons[ic % 2] = P.cnt["pe"]
                down_pe = P.cnt["pe"]

                for d6 in range(nd):
                    doc = d0 + d6
                    par = doc % 2
                    x2sem = "lx0a" if par == 0 else "lx0b"

                    def ldx3(E, doc=doc, tc0=tc0, par=par):
                        return E.dma_start(
                            out=x2c[par][:],
                            in_=x2T[doc * 128:(doc + 1) * 128, tc0:tc0 + NT])
                    ev_wl = [("pe", down_pe)]

                    def evr4(E, doc=doc, d6=d6):
                        return E.tensor_tensor(
                            out=rbuf[:, doc * NT:(doc + 1) * NT],
                            in0=psD[d6][:],
                            in1=x3buf[:, doc * NT:(doc + 1) * NT], op=ALU.add)
                    P.add("vector", evr4, waits=ev_wl)
                    ev_dv = P.cnt["dv"]

                    def sqr4(E, doc=doc):
                        return E.activation(
                            out=r2t[:], in_=rbuf[:, doc * NT:(doc + 1) * NT],
                            func=AF.Square)
                    P.add("scalar", sqr4,
                      waits=[("dv", ev_dv), ("pe", P.cnt["pe"])])
                    sq_ac = P.cnt["ac"]

                    def st1d(E, doc=doc):
                        return E.matmul(psS1[0:1, 0:NT], on1[:],
                                        rbuf[:, doc * NT:(doc + 1) * NT],
                                        start=(doc == 0), stop=(doc == 31))
                    P.add("tensor", st1d, waits=[("dv", ev_dv)])

                    def st2d(E, doc=doc):
                        return E.matmul(psS2[0:1, 0:NT], on1[:], r2t[:],
                                        start=(doc == 0), stop=(doc == 31))
                    P.add("tensor", st2d, waits=[("ac", sq_ac)])

            rstd_ac = ln_rows()
            bc_pe = bcast_rows(rstd_ac)
            for doc in range(32):
                def nrm1f(E, doc=doc):
                    return E.tensor_tensor(
                        out=tn[:], in0=rbuf[:, doc * NT:(doc + 1) * NT],
                        in1=psD[0][:], op=ALU.subtract)
                P.add("vector", nrm1f, waits=([("pe", bc_pe)] if doc == 0 else []))

                def nrm2f(E, doc=doc):
                    return E.tensor_tensor(
                        out=xf[:], in0=tn[:], in1=psD[1][:], op=ALU.mult)
                P.add("vector", nrm2f, waits=[("st", P.cnt["st"])])
                n2_dv = P.cnt["dv"]

                def stx(E, doc=doc, tc0=tc0):
                    return E.dma_start(
                        out=xout[doc * 128:(doc + 1) * 128, tc0:tc0 + NT],
                        in_=xf[:])
                P.add("scalar", stx, waits=[("dv", n2_dv)], inc="st", inck=16)

        P.ops["scalar"].append(
            (lambda E: E.wait_ge(P.sems["st"], P.cnt["st"]), [], None, 1))
        P.check()
        block = e(nc.Block())
        P.emit(block)
    return nc


def launch_E_inputs(inputs, oT1_glob, x2_glob, wg, wu, wd):
    wo1t = hosttile(np.ascontiguousarray(
        np.asarray(inputs["L1_attn_Wo"], np.float32).T), 128)
    wgt = np.ascontiguousarray(
        np.asarray(wg).reshape(32, 128, 86, 128).transpose(1, 2, 0, 3)
        .reshape(128, 86 * 32 * 128))
    wut = np.ascontiguousarray(
        np.asarray(wu).reshape(32, 128, 86, 128).transpose(1, 2, 0, 3)
        .reshape(128, 86 * 32 * 128))
    wdtl = np.ascontiguousarray(
        np.asarray(wd).reshape(86, 128, 32, 128).transpose(1, 0, 2, 3)
        .reshape(128, 86 * 32 * 128))
    onesb = np.ones((128, 1), BF)
    onesf = np.ones((1, 512), np.float32)
    epsr = np.full((1, 1), 1e-5, np.float32)
    maps = []
    for c in range(NCORES):
        t0 = c * TOK
        m = dict(
            x2T=np.ascontiguousarray(np.asarray(x2_glob)[t0:t0 + TOK].T
                                     .astype(np.float32)),
            oTm=np.ascontiguousarray(np.asarray(oT1_glob)[:, t0:t0 + TOK]),
            wo1=wo1t, wgt=wgt, wut=wut, wdt=wdtl,
            onesb=onesb, onesf=onesf, epsr=epsr,
        )
        maps.append(m)
    return maps



_prog_cache = {}


def _prog(name, builder):
    if name not in _prog_cache:
        _prog_cache[name] = builder()
    return _prog_cache[name]


def kernel(**inputs):
    inputs = {k: np.asarray(v) for k, v in inputs.items()}

    resA = run_bass_kernel_spmd(
        _prog("A", build_launch_A), launch_A_inputs(inputs),
        core_ids=list(range(NCORES))).results
    q0 = np.concatenate([resA[c]["qo"] for c in range(NCORES)], axis=0)
    k0 = np.concatenate([resA[c]["ko"] for c in range(NCORES)], axis=0)
    v0 = np.concatenate([resA[c]["vo"] for c in range(NCORES)], axis=0)
    wg = np.concatenate([resA[c]["wgb"] for c in range(NCORES)], axis=1)
    wu = np.concatenate([resA[c]["wub"] for c in range(NCORES)], axis=1)
    wd = np.concatenate([resA[c]["wdb"] for c in range(NCORES)], axis=0)[0:I_FFN]

    resB = run_bass_kernel_spmd(
        _prog("B", lambda: build_launch_attn(0)),
        launch_attn_inputs(0, q0, k0, v0),
        core_ids=list(range(NCORES))).results
    oT0 = np.zeros((D, B * S), BF)
    for c in range(NCORES):
        b = c // 4
        h0 = 8 * (c % 4)
        oT0[h0 * HD:(h0 + 8) * HD, b * S:(b + 1) * S] = resB[c]["oT"]

    resC = run_bass_kernel_spmd(
        _prog("C", build_launch_C), launch_C_inputs(inputs, oT0),
        core_ids=list(range(NCORES))).results
    x2 = np.concatenate([resC[c]["x2T"].T for c in range(NCORES)], axis=0)
    q1 = np.concatenate([resC[c]["q1o"] for c in range(NCORES)], axis=0)
    k1 = np.concatenate([resC[c]["k1o"] for c in range(NCORES)], axis=0)
    v1 = np.concatenate([resC[c]["v1o"] for c in range(NCORES)], axis=0)

    resD = run_bass_kernel_spmd(
        _prog("D", lambda: build_launch_attn(1)),
        launch_attn_inputs(1, q1, k1, v1),
        core_ids=list(range(NCORES))).results
    oT1 = np.zeros((D, B * S), BF)
    for c in range(NCORES):
        b = c // 4
        h0 = 8 * (c % 4)
        oT1[h0 * HD:(h0 + 8) * HD, b * S:(b + 1) * S] = resD[c]["oT"]

    resE = run_bass_kernel_spmd(
        _prog("E", build_launch_E), launch_E_inputs(inputs, oT1, x2, wg, wu, wd),
        core_ids=list(range(NCORES))).results
    out = np.concatenate([resE[c]["xout"].T for c in range(NCORES)], axis=0)
    return np.ascontiguousarray(out.reshape(B, S, D).astype(np.float32))


# version 10
# speedup vs baseline: 1.0588x; 1.0588x over previous
import math
import numpy as np
import ml_dtypes
from contextlib import ExitStack

import concourse.bass as bass
import concourse.mybir as mybir
from concourse.bass_utils import run_bass_kernel_spmd

F32 = mybir.dt.float32
BF16 = mybir.dt.bfloat16
I32 = mybir.dt.int32
AF = mybir.ActivationFunctionType
ALU = mybir.AluOpType
BF = ml_dtypes.bfloat16

NCORES = 8
B = 2
S = 4096
D = 4096
NH = 32
NKV = 8
HD = 128
TOK = 1024
I_FFN = 11008
NG_GU = 32
NG_D = 11
GOFF = [0, 1408, 2816, 4224, 5632, 7040, 8448, 9728]
GW = [1408] * 6 + [1280] * 2
GROUPS8 = [list(range(8))]
GROUPS4 = [[0, 1, 2, 3], [4, 5, 6, 7]]


def t2cl(t):
    if t < 66:
        return t // 11, t % 11
    if t < 76:
        return 6, t - 66
    return 7, t - 76



class Plan:
    ENGS = ("sync", "gpsimd", "tensor", "vector", "scalar")
    ENG_SEM = {"tensor": "pe", "vector": "dv", "scalar": "ac"}

    def __init__(self, nc):
        self.nc = nc
        self.ops = {k: [] for k in self.ENGS}
        self.emitted = {k: 0 for k in self.ENGS}
        self.sems = {}
        self.cnt = {}
        self.pending = {}

    def sem(self, ctx, name):
        h = ctx.enter_context(self.nc.semaphore(name))
        self.sems[name] = h
        self.cnt[name] = 0
        return h

    def add(self, eng, fn, waits=(), inc=None, inck=1):
        waits = list(self.pending.pop(eng, ())) + list(waits)
        es = self.ENG_SEM.get(eng)
        if es is not None and es in self.cnt:
            if eng != "tensor":
                waits.append((es, self.cnt[es]))
            if inc is None:
                inc = es
        self.ops[eng].append((fn, waits, inc, inck))
        if inc is not None:
            self.cnt[inc] += (1 if inck is None else inck)
            return self.cnt[inc]
        return None

    def barrier(self, names=("pe", "ac", "dv", "st", "sth", "ws")):
        snap = [(n, self.cnt[n]) for n in names if n in self.cnt]
        for eng in self.ENGS:
            self.pending[eng] = snap + list(self.pending.get(eng, ()))

    def check(self):
        idx = {k: 0 for k in self.ENGS}
        cur = {k: 0 for k in self.cnt}
        total = len(sum(self.ops.values(), []))
        done = 0
        progress = True
        while progress:
            progress = False
            for eng in self.ENGS:
                while idx[eng] < len(self.ops[eng]):
                    fn, waits, inc, inck = self.ops[eng][idx[eng]]
                    if all(cur[sn] >= v for sn, v in waits):
                        if inc is not None:
                            cur[inc] += (1 if inck is None else inck)
                        idx[eng] += 1
                        done += 1
                        progress = True
                    else:
                        break
        if done < total:
            for eng in self.ENGS:
                if idx[eng] < len(self.ops[eng]):
                    fn, waits, inc, inck = self.ops[eng][idx[eng]]
                    bad = [(sn, v, cur[sn]) for sn, v in waits if cur[sn] < v]
                    print(f"DEADLOCK {eng} op#{idx[eng]}: needs {bad}")
            raise AssertionError(f"plan deadlock: {done}/{total} ops runnable")
        return True

    def flush(self, block):
        for eng in self.ENGS:
            ops = self.ops[eng][self.emitted[eng]:]
            self.emitted[eng] = len(self.ops[eng])
            if not ops:
                continue
            sems = self.sems

            def make(eng_ops):
                def body(engine):
                    for fn, waits, inc, inck in eng_ops:
                        for sname, val in waits:
                            engine.wait_ge(sems[sname], val)
                        ins = fn(engine)
                        if inc is not None:
                            if inck is None:
                                ins.then_inc(sems[inc])
                            else:
                                ins.then_inc(sems[inc], inck)
                return body
            getattr(block, eng)(make(ops))


def agop(P, sem, src, dst, groups, waits):
    def cc(E, src=src, dst=dst, groups=groups):
        return E.collective_compute(
            "AllGather", ALU.bypass, replica_groups=groups,
            ins=[src.opt()], outs=[dst.opt()])
    P.add("gpsimd", cc, waits=waits, inc=sem, inck=None)


def rsop(P, sem, src, dst, groups, waits):
    def cc(E, src=src, dst=dst, groups=groups):
        return E.collective_compute(
            "ReduceScatter", ALU.add, replica_groups=groups,
            ins=[src.opt()], outs=[dst.opt()])
    P.add("gpsimd", cc, waits=waits, inc=sem, inck=None)



def sec_proj(nc, P, e, T, layer):
    pf = "a" if layer == 0 else "c3"
    wsl = e(nc.sbuf_tensor(pf + "wsl", [128, 32 * 1536], BF16))
    xt = [e(nc.sbuf_tensor(f"{pf}xt{i}", [128, 32 * 128], BF16)) for i in range(2)]
    cw = 64 if layer == 0 else 128
    cosb = e(nc.sbuf_tensor(pf + "cos", [128, 32 * cw], F32))
    sinb = e(nc.sbuf_tensor(pf + "sin", [128, 32 * cw], F32))
    t1 = e(nc.sbuf_tensor(pf + "t1", [128, 512], F32))
    t2 = e(nc.sbuf_tensor(pf + "t2", [128, 256], F32))
    pq = [e(nc.sbuf_tensor(f"{pf}pq{i}", [128, 512], BF16)) for i in range(2)]
    tevb = [e(nc.sbuf_tensor(f"{pf}tv{i}", [128, 512], BF16)) for i in range(2)]
    idb = e(nc.sbuf_tensor(pf + "id", [128, 128], BF16))
    ps = [e(nc.psum_tensor(f"{pf}ps{i}", [128, 512], F32)) for i in range(2)]
    psT = e(nc.psum_tensor(pf + "psT", [128, 512], BF16))

    wsrc = T["wqkv0"] if layer == 0 else T["wqkv1"]
    cossrc = T["cos0"] if layer == 0 else T["cos1"]
    sinsrc = T["sin0"] if layer == 0 else T["sin1"]
    qdst = T["q0T"] if layer == 0 else T["q1T"]
    kdst = T["k0T"] if layer == 0 else T["k1T"]
    vdst = T["v0"] if layer == 0 else T["v1"]

    ld0 = P.cnt["ld"]
    P.add("sync", lambda E: E.dma_start(
        out=cosb[:].rearrange("p (tc f) -> p tc f", f=cw),
        in_=cossrc.rearrange("(tc p) f -> p tc f", p=128)), inc="ld", inck=16)
    P.add("sync", lambda E: E.dma_start(
        out=sinb[:].rearrange("p (tc f) -> p tc f", f=cw),
        in_=sinsrc.rearrange("(tc p) f -> p tc f", p=128)), inc="ld", inck=16)
    P.add("sync", lambda E: E.dma_start(out=idb[:], in_=T["idn"]),
          inc="ld", inck=16)
    LD_ALL = P.cnt["ld"]

    ldg0 = P.cnt["ldg"]
    P.add("gpsimd", lambda E: E.dma_start(
        out=wsl[:].rearrange("p (kc n) -> p kc n", n=1536),
        in_=wsrc.rearrange("(kc p) n -> p kc n", p=128)), inc="ldg", inck=16)
    LDG_ALL = P.cnt["ldg"]

    if layer == 0:
        def mkxin(E, tt):
            return T["xTb"][:, tt * 128:(tt + 1) * 128] \
                .rearrange("(kc p) t -> p kc t", p=128)
    else:
        def mkxin(E, tt):
            r = tt // 8
            c0 = (tt % 8) * 128
            return T["x2g"].rearrange("(z p) t -> p z t", p=128) \
                [:, r * 32:(r + 1) * 32, c0:c0 + 128]

    rope_dv = {}
    pqfree = {0: [], 1: []}
    xcons = [0, 0]
    psT_free = P.cnt["ac"]
    tev_store = {0: P.cnt["st"], 1: P.cnt["st"]}

    for tt in range(32):
        def ldx(E, tt=tt, par=tt % 2):
            return E.dma_start(
                out=xt[par][:].rearrange("p (kc t) -> p kc t", t=128),
                in_=mkxin(E, tt))
        xwl = [("pe", xcons[tt % 2])]
        if layer == 1 and tt == 0:
            xwl.append(("ccx2", 1))
        P.add("gpsimd", ldx, waits=xwl, inc=f"lw{tt % 2}", inck=16)
        xl = (f"lw{tt % 2}", P.cnt[f"lw{tt % 2}"])

        for ncix in range(3):
            g = tt * 3 + ncix
            for kc in range(32):
                wl = []
                if kc == 0:
                    wl = [xl]
                    if g == 0:
                        wl += [("ld", LD_ALL), ("ldg", LDG_ALL)]
                    if g >= 2:
                        wl.append(("dv", rope_dv[g - 2]))

                def mmp(E, kc=kc, ncix=ncix, par=tt % 2, po=g % 2):
                    return E.matmul(
                        ps[po][:], xt[par][:, kc * 128:(kc + 1) * 128],
                        wsl[:, kc * 1536 + ncix * 512:kc * 1536 + ncix * 512 + 512],
                        start=(kc == 0), stop=(kc == 31))
                P.add("tensor", mmp, waits=wl)
            mm_pe = P.cnt["pe"]

            po = g % 2
            pb = ps[po].ap().rearrange("p (h f) -> p h f", f=128)
            ob = pq[po].ap().rearrange("p (h f) -> p h f", f=128)
            nhr = 4 if ncix < 2 else 2
            rw = [("pe", mm_pe)] + list(pqfree[po])
            if layer == 0:
                c_ap = cosb[:, tt * 64:(tt + 1) * 64].unsqueeze(1) \
                    .to_broadcast([128, nhr, 64])
                s_ap = sinb[:, tt * 64:(tt + 1) * 64].unsqueeze(1) \
                    .to_broadcast([128, nhr, 64])
                pe_ = pb[:, 0:nhr, 0:128:2]
                po_ = pb[:, 0:nhr, 1:128:2]
                oe = ob[:, 0:nhr, 0:128:2]
                oo = ob[:, 0:nhr, 1:128:2]
                t1v = t1.ap().rearrange("p (h f) -> p h f", f=64)[:, 0:nhr, :]
                t2v = t2.ap().rearrange("p (h f) -> p h f", f=64)[:, 0:nhr, :]
                P.add("vector", lambda E, a=pe_, b_=c_ap, o=t1v:
                      E.tensor_tensor(out=o, in0=a, in1=b_, op=ALU.mult),
                      waits=rw)
                P.add("vector", lambda E, a=po_, b_=s_ap, o=t2v:
                      E.tensor_tensor(out=o, in0=a, in1=b_, op=ALU.mult))
                P.add("vector", lambda E, a=t1v, b_=t2v, o=oe:
                      E.tensor_tensor(out=o, in0=a, in1=b_, op=ALU.subtract))
                P.add("vector", lambda E, a=pe_, b_=s_ap, o=t1v:
                      E.tensor_tensor(out=o, in0=a, in1=b_, op=ALU.mult))
                P.add("vector", lambda E, a=po_, b_=c_ap, o=t2v:
                      E.tensor_tensor(out=o, in0=a, in1=b_, op=ALU.mult))
                P.add("vector", lambda E, a=t1v, b_=t2v, o=oo:
                      E.tensor_tensor(out=o, in0=a, in1=b_, op=ALU.add))
            else:
                c3 = cosb[:, tt * 128:(tt + 1) * 128].unsqueeze(1) \
                    .to_broadcast([128, nhr, 128])
                s3 = sinb[:, tt * 128:(tt + 1) * 128].unsqueeze(1) \
                    .to_broadcast([128, nhr, 128])
                pv = pb[:, 0:nhr, :]
                ov = ob[:, 0:nhr, :]
                t1v = t1.ap().rearrange("p (h f) -> p h f", f=128)[:, 0:nhr, :]
                t2v = t2.ap().rearrange("p (h f) -> p h f", f=64)[:, 0:nhr, :]
                P.add("vector", lambda E, pv=pv, c3=c3, t1v=t1v:
                      E.tensor_tensor(out=t1v, in0=pv, in1=c3, op=ALU.mult),
                      waits=rw)
                P.add("vector", lambda E, pv=pv, s3=s3, t2v=t2v:
                      E.tensor_tensor(out=t2v, in0=pv[:, :, 64:128],
                                      in1=s3[:, :, 0:64], op=ALU.mult))
                P.add("vector", lambda E, ov=ov, t1v=t1v, t2v=t2v:
                      E.tensor_tensor(out=ov[:, :, 0:64], in0=t1v[:, :, 0:64],
                                      in1=t2v, op=ALU.subtract))
                P.add("vector", lambda E, pv=pv, s3=s3, t2v=t2v:
                      E.tensor_tensor(out=t2v, in0=pv[:, :, 0:64],
                                      in1=s3[:, :, 64:128], op=ALU.mult))
                P.add("vector", lambda E, ov=ov, t1v=t1v, t2v=t2v:
                      E.tensor_tensor(out=ov[:, :, 64:128],
                                      in0=t1v[:, :, 64:128], in1=t2v,
                                      op=ALU.add))
            if ncix == 2:
                P.add("vector", lambda E, po=po:
                      E.tensor_copy(out=pq[po][:, 256:512],
                                    in_=ps[po][:, 256:512]))
            rope_dv[g] = P.cnt["dv"]

            nq = 4 if ncix < 2 else 2
            for q in range(nq):
                twl = [("dv", rope_dv[g])] if q == 0 else []
                if q == 0:
                    twl.append(("ac", psT_free))

                def trq(E, q=q, po=po):
                    return E.transpose(psT[:, q * 128:(q + 1) * 128],
                                       pq[po][:, q * 128:(q + 1) * 128],
                                       idb[:])
                P.add("tensor", trq, waits=twl)
            tr_pe = P.cnt["pe"]

            def tev(E, po=po, nq=nq):
                return E.activation(out=tevb[po][:, 0:nq * 128],
                                    in_=psT[:, 0:nq * 128], func=AF.Copy)
            P.add("scalar", tev, waits=[("pe", tr_pe), ("st", tev_store[po])])
            psT_free = P.cnt["ac"]
            ev_ac = P.cnt["ac"]

            if ncix < 2:
                def stq(E, tt=tt, ncix=ncix, po=po):
                    return E.dma_start(
                        out=qdst.rearrange("p (h t) -> p h t", t=S)
                        [:, ncix * 4:(ncix + 1) * 4, tt * 128:(tt + 1) * 128],
                        in_=tevb[po][:].rearrange("p (q t) -> p q t", t=128))
                P.add("scalar", stq, waits=[("ac", ev_ac)], inc="st", inck=16)
                tev_store[po] = P.cnt["st"]
                pqfree[po] = [("pe", tr_pe)]
            else:
                def stk(E, tt=tt, po=po):
                    return E.dma_start(
                        out=kdst.rearrange("p (h t) -> p h t", t=S)
                        [:, 0:2, tt * 128:(tt + 1) * 128],
                        in_=tevb[po][:, 0:256].rearrange(
                            "p (q t) -> p q t", t=128))
                P.add("scalar", stk, waits=[("ac", ev_ac)], inc="st", inck=16)
                tev_store[po] = P.cnt["st"]

                def stv(E, tt=tt, po=po):
                    return E.dma_start(
                        out=vdst[tt * 128:(tt + 1) * 128, 0:256],
                        in_=pq[po][:, 256:512])
                P.add("scalar", stv, waits=[("dv", rope_dv[g])],
                      inc="st", inck=16)
                pqfree[po] = [("pe", tr_pe), ("st", P.cnt["st"])]
        xcons[tt % 2] = P.cnt["pe"]



def sec_dequant(nc, P, e, T):
    zrow = [e(nc.sbuf_tensor(f"dqz{p}", [NG_GU, 512], I32)) for p in range(3)]
    zf = e(nc.sbuf_tensor("dqzf", [NG_GU, 4096], I32))
    zf32 = e(nc.sbuf_tensor("dqzf32", [NG_GU, 4096], F32))
    qslab = e(nc.sbuf_tensor("dqqs", [128, 512], I32))
    nibi = e(nc.sbuf_tensor("dqni", [128, 2048], I32))
    nibf = e(nc.sbuf_tensor("dqnf", [128, 2048], BF16))
    wslab = e(nc.sbuf_tensor("dqws", [128, 2048], BF16))
    zbc = e(nc.sbuf_tensor("dqzb", [128, 2048], BF16))
    sbc = e(nc.sbuf_tensor("dqsb", [128, 2048], BF16))

    dplans = [
        (T["gq"], T["gs"], T["slabg"], NG_GU, 176, 1408, 0, False),
        (T["uq"], T["us"], T["slabu"], NG_GU, 176, 1408, NG_GU, False),
        (T["dqw"], T["ds"], T["slabd"], NG_D, 512, 4096, 2 * NG_GU, True),
    ]
    zl0 = P.cnt["ld"]
    for p, (qsrc, ssrc, wdst, ngr, packw, outw, zr0, isdn) in enumerate(dplans):
        zsrc = [T["gz"], T["uz"], T["dz"]][p]
        P.add("sync", lambda E, p=p, ngr=ngr, packw=packw, zsrc=zsrc:
              E.dma_start(out=zrow[p][0:ngr, 0:packw], in_=zsrc),
              inc="ld", inck=16)
    ZL_ALL = P.cnt["ld"]

    gb_expected = []
    gcum_g = 0
    for p, (qsrc, ssrc, wdst, ngr, packw, outw, zr0, isdn) in enumerate(dplans):
        P.add("gpsimd", lambda E, zr0=zr0, ngr=ngr, outw=outw:
              E.dma_start(out=T["zscr"][zr0:zr0 + ngr, 0:outw],
                          in_=zf32[0:ngr, 0:outw]),
              waits=[("vz", p + 1)], inc="zw", inck=16)
        zw_here = P.cnt["zw"]
        nh = (outw + 2047) // 2048
        for g in range(ngr):
            for h2 in range(nh):
                c0 = h2 * 2048
                cwd = min(2048, outw - c0)
                wl = [("zw", zw_here)]
                if gcum_g > 0:
                    wl.append(("vd", gcum_g))
                P.add("gpsimd", lambda E, zr0=zr0, g=g, c0=c0, cwd=cwd:
                      E.dma_start(
                          out=zbc[:, 0:cwd].unsqueeze(1),
                          in_=T["zscr"][zr0 + g:zr0 + g + 1, c0:c0 + cwd]
                          .partition_broadcast(128)),
                      waits=wl, inc="gb", inck=16)
                if h2 == 0:
                    P.add("gpsimd", lambda E, qsrc=qsrc, g=g, packw=packw:
                          E.dma_start(out=qslab[:, 0:packw],
                                      in_=qsrc[g * 128:(g + 1) * 128, 0:packw]),
                          inc="gb", inck=16)
                P.add("gpsimd", lambda E, ssrc=ssrc, g=g, c0=c0, cwd=cwd:
                      E.dma_start(
                          out=sbc[:, 0:cwd].unsqueeze(1),
                          in_=ssrc[g:g + 1, c0:c0 + cwd]
                          .partition_broadcast(128)),
                      inc="gb", inck=16)
                gb_expected.append(P.cnt["gb"])
                gcum_g += 1

    gcum = 0
    for p, (qsrc, ssrc, wdst, ngr, packw, outw, zr0, isdn) in enumerate(dplans):
        wl = [("ld", ZL_ALL)]
        if p > 0:
            wl.append(("zw", 16 * p))
        for i in range(8):
            P.add("vector", lambda E, p=p, i=i, ngr=ngr, packw=packw, outw=outw:
                  E.tensor_scalar(
                      out=zf[0:ngr, i:outw:8], in0=zrow[p][0:ngr, 0:packw],
                      scalar1=float(4 * i), scalar2=15.0,
                      op0=ALU.logical_shift_right, op1=ALU.bitwise_and),
                  waits=(wl if i == 0 else []))
        P.add("vector", lambda E, ngr=ngr, outw=outw:
              E.tensor_copy(out=zf32[0:ngr, 0:outw], in_=zf[0:ngr, 0:outw]),
              inc="vz")
        nh = (outw + 2047) // 2048
        for g in range(ngr):
            for h2 in range(nh):
                c0 = h2 * 2048
                cwd = min(2048, outw - c0)
                pc0 = c0 // 8
                pcw = cwd // 8
                wl = [("gb", gb_expected[gcum])]
                if gcum >= 1:
                    wl.append(("ws", 16 * gcum))
                for i in range(8):
                    P.add("vector", lambda E, i=i, cwd=cwd, pc0=pc0, pcw=pcw:
                          E.tensor_scalar(
                              out=nibi[:, i:cwd:8], in0=qslab[:, pc0:pc0 + pcw],
                              scalar1=float(4 * i), scalar2=15.0,
                              op0=ALU.logical_shift_right, op1=ALU.bitwise_and),
                          waits=(wl if i == 0 else []))
                P.add("vector", lambda E, cwd=cwd:
                      E.tensor_copy(out=nibf[:, 0:cwd], in_=nibi[:, 0:cwd]))
                P.add("vector", lambda E, cwd=cwd:
                      E.tensor_tensor(out=wslab[:, 0:cwd], in0=nibf[:, 0:cwd],
                                      in1=zbc[:, 0:cwd], op=ALU.subtract))
                P.add("vector", lambda E, cwd=cwd:
                      E.tensor_tensor(out=wslab[:, 0:cwd], in0=wslab[:, 0:cwd],
                                      in1=sbc[:, 0:cwd], op=ALU.mult),
                      inc="vd")
                vd_here = P.cnt["vd"]

                if isdn:
                    def stw(E, wdst=wdst, g=g, c0=c0, cwd=cwd):
                        return E.dma_start(
                            out=wdst[:, g * 4096 + c0:g * 4096 + c0 + cwd],
                            in_=wslab[:, 0:cwd])
                else:
                    def stw(E, wdst=wdst, g=g):
                        return E.dma_start(
                            out=wdst.rearrange("p (l x) -> p l x", x=4096)
                            [:, 0:11, g * 128:(g + 1) * 128],
                            in_=wslab[:, 0:1408].rearrange(
                                "p (l j) -> p l j", j=128))
                P.add("scalar", stw, waits=[("vd", vd_here)], inc="ws", inck=16)
                gcum += 1




def sec_attn(nc, P, e, T, layer):
    pf = "b" if layer == 0 else "d"
    qT = T["q0T"] if layer == 0 else T["q1T"]
    kT = T["k0T"] if layer == 0 else T["k1T"]
    vt = T["v0"] if layer == 0 else T["v1"]
    RSin = T["RSin0"] if layer == 0 else T["RSin1"]
    oTm = T["oTm0"] if layer == 0 else T["oTm1"]

    NQT = 32
    NITER = 14
    B0 = 16.0

    kb = e(nc.sbuf_tensor(pf + "kb", [128, 2 * S], BF16))
    vb = e(nc.sbuf_tensor(pf + "vb", [128, 32 * 2 * HD], BF16))
    qt = [e(nc.sbuf_tensor(f"{pf}qt{i}", [128, 512], BF16)) for i in range(2)]
    idb = e(nc.sbuf_tensor(pf + "id", [128, 128], BF16))
    trib = e(nc.sbuf_tensor(pf + "tri", [128, 128], F32))
    tritb = e(nc.sbuf_tensor(pf + "trt", [128, 128], F32))
    tribb = e(nc.sbuf_tensor(pf + "trbb", [128, 128], BF16))
    cstb = e(nc.sbuf_tensor(pf + "cst", [128, 8], F32))
    on1 = e(nc.sbuf_tensor(pf + "on1", [128, 1], BF16))
    idfb = e(nc.sbuf_tensor(pf + "idf", [128, 128], F32))
    onf = e(nc.sbuf_tensor(pf + "onf", [1, 128], F32))
    mvb = e(nc.sbuf_tensor(pf + "mv", [128, 4], F32))
    pbuf = [e(nc.sbuf_tensor(f"{pf}pb{i}", [128, 512], BF16)) for i in range(3)]
    zrow = e(nc.sbuf_tensor(pf + "zr", [1, 512], F32))
    zir = e(nc.sbuf_tensor(pf + "zi", [1, 512], F32))
    osb = e(nc.sbuf_tensor(pf + "osb", [128, 512], BF16))
    onrm = e(nc.sbuf_tensor(pf + "onr", [128, 512], BF16))
    osb2 = [e(nc.sbuf_tensor(f"{pf}os2{i}", [128, 512], BF16)) for i in range(2)]
    if layer == 1:
        sq = [e(nc.sbuf_tensor(f"{pf}sq{i}", [128, S], F32)) for i in range(4)]
        ebuf = [e(nc.sbuf_tensor(f"{pf}eb{i}", [128, 512], BF16)) for i in range(3)]
        junkb = e(nc.sbuf_tensor(pf + "jk", [128, S], BF16))
        tgtb = e(nc.sbuf_tensor(pf + "tg", [128, 32], F32))
        tg2b = e(nc.sbuf_tensor(pf + "tg2", [128, 32], F32))
        fmulb = e(nc.sbuf_tensor(pf + "fm", [128, 32], F32))
        faddb = e(nc.sbuf_tensor(pf + "fa", [128, 32], F32))
        faddnb = e(nc.sbuf_tensor(pf + "fan", [128, 32], F32))
        steprb = e(nc.sbuf_tensor(pf + "sp", [128, 24], F32))
        thr = e(nc.sbuf_tensor(pf + "th", [128, 4], F32))
        thr2 = e(nc.sbuf_tensor(pf + "th2", [128, 4], F32))
        thr2b = e(nc.sbuf_tensor(pf + "th2b", [128, 1], F32))
        junkb2 = e(nc.sbuf_tensor(pf + "jk2", [128, S], BF16))
        trow = e(nc.sbuf_tensor(pf + "tro", [1, 512], F32))
        tmatb = e(nc.sbuf_tensor(pf + "tm", [128, 512], F32))
        mb = e(nc.sbuf_tensor(pf + "mb", [128, 512], BF16))

    psST = [e(nc.psum_tensor(f"{pf}pST{i}", [128, 512], F32)) for i in range(3)]
    psO = e(nc.psum_tensor(pf + "psO", [128, 512], F32))
    psZ = e(nc.psum_tensor(pf + "psZ", [128, 512], F32))
    if layer == 0:
        psIZ = e(nc.psum_tensor(pf + "psIZ", [128, 512], F32))
    if layer == 1:
        psSq = [e(nc.psum_tensor(f"{pf}pSq{i}", [128, 512], F32)) for i in range(2)]
        psTR = e(nc.psum_tensor(pf + "psTR", [128, 512], F32))
        psIZ = psSq[0]

    jobs = [(u, g) for u in range(2) for g in range(NQT)]

    ld0 = P.cnt["ld"]
    P.add("sync", lambda E: E.dma_start(out=kb[:], in_=kT), inc="ld", inck=16)
    P.add("sync", lambda E: E.dma_start(
        out=vb[:].rearrange("p (kc n) -> p kc n", n=2 * HD),
        in_=vt.rearrange("(kc p) n -> p kc n", p=128)), inc="ld", inck=16)
    P.add("sync", lambda E: E.dma_start(out=idb[:], in_=T["idn"]), inc="ld", inck=16)
    P.add("sync", lambda E: E.dma_start(out=trib[:], in_=T["tri"]), inc="ld", inck=16)
    P.add("sync", lambda E: E.dma_start(out=tritb[:], in_=T["trit"]), inc="ld", inck=16)
    P.add("sync", lambda E: E.dma_start(out=cstb[:], in_=T["cst"]), inc="ld", inck=16)
    P.add("sync", lambda E: E.dma_start(out=on1[:], in_=T["onesc"]), inc="ld", inck=16)
    P.add("sync", lambda E: E.dma_start(out=idfb[:], in_=T["idf"]), inc="ld", inck=16)
    P.add("sync", lambda E: E.dma_start(out=onf[:], in_=T["onfr"]), inc="ld", inck=16)
    P.add("sync", lambda E: E.dma_start(out=mvb[:], in_=T["mv"]), inc="ld", inck=16)
    if layer == 1:
        P.add("sync", lambda E: E.dma_start(out=tgtb[:], in_=T["tgt"]), inc="ld", inck=16)
        P.add("sync", lambda E: E.dma_start(out=tg2b[:], in_=T["tg2"]), inc="ld", inck=16)
        P.add("sync", lambda E: E.dma_start(out=fmulb[:], in_=T["fmul"]), inc="ld", inck=16)
        P.add("sync", lambda E: E.dma_start(out=faddb[:], in_=T["fadd"]), inc="ld", inck=16)
        P.add("sync", lambda E: E.dma_start(out=faddnb[:], in_=T["faddn"]), inc="ld", inck=16)
        P.add("sync", lambda E: E.dma_start(out=steprb[:], in_=T["stepr"]), inc="ld", inck=16)
    LD_ALL = P.cnt["ld"]
    if layer == 1:
        P.add("vector", lambda E: E.tensor_copy(out=tribb[:], in_=trib[:]),
              waits=[("ld", LD_ALL)])

    lq0_0 = P.cnt["lq0"]
    lq1_0 = P.cnt["lq1"]

    def plan_qload(j, waits):
        u, g = jobs[j]

        def qload(E, u=u, g=g, j=j):
            return E.dma_start(
                out=qt[j % 2][:].rearrange("p (h t) -> p h t", t=128),
                in_=qT.rearrange("p (h t) -> p h t", t=S)
                [:, u * 4:(u + 1) * 4, g * 128:(g + 1) * 128])
        P.add("sync", qload, waits=waits, inc=f"lq{j % 2}", inck=16)

    plan_qload(0, [])
    plan_qload(1, [])

    ev_sqev = []
    prev_cnt_dv = 0
    prev_cnt_ac = 0
    ev_pP = {}
    ev_pcons = {}
    ev_expc = []
    last_oevac = 0
    last_tmat_dv = 0
    last_zevac = 0
    last_izmul = 0
    last_store = P.cnt["st"]
    pgi = 0

    for j, (u, g) in enumerate(jobs):
        ext = (g + 1) * 128
        nkc = g + 1
        ngrp = (ext + 511) // 512
        lqbase = lq0_0 if j % 2 == 0 else lq1_0
        first_wait = [(f"lq{j % 2}", lqbase + 16 * (j // 2 + 1))]
        if j == 0:
            first_wait.append(("ld", LD_ALL))

        if layer == 1:
            sqi = 0
            ev_tria = {}
            for hl in range(4):
                for ci in range(ngrp):
                    n0 = ci * 512
                    nw = min(512, ext - n0)
                    wl = list(first_wait) if (hl == 0 and ci == 0) else []
                    if len(ev_sqev) >= 2:
                        wl.append(ev_sqev[-2])
                    if j >= 1 and hl == 0 and ci == 0:
                        wl.append(("dv", prev_cnt_dv))
                        wl.append(("ac", prev_cnt_ac))
                        wl.append(("dv", last_izmul))

                    def mmsq(E, j=j, u=u, hl=hl, n0=n0, nw=nw, par=sqi % 2):
                        return E.matmul(
                            psSq[par][:, 0:nw],
                            qt[j % 2][:, hl * 128:(hl + 1) * 128],
                            kb[:, u * S + n0:u * S + n0 + nw],
                            start=True, stop=True)
                    P.add("tensor", mmsq, waits=wl)
                    mm_pe = P.cnt["pe"]

                    if hl % 2 == 0:
                        def sqevd(E, n0=n0, nw=nw, par=sqi % 2, hl=hl):
                            return E.tensor_copy(
                                out=sq[hl][:, n0:n0 + nw],
                                in_=psSq[par][:, 0:nw])
                        P.add("vector", sqevd, waits=[("pe", mm_pe)])
                        ev_sqev.append(("dv", P.cnt["dv"]))
                    else:
                        def sqeva(E, n0=n0, nw=nw, par=sqi % 2, hl=hl):
                            return E.activation(
                                out=sq[hl][:, n0:n0 + nw],
                                in_=psSq[par][:, 0:nw], func=AF.Copy)
                        P.add("scalar", sqeva, waits=[("pe", mm_pe)])
                        ev_sqev.append(("ac", P.cnt["ac"]))
                    sqi += 1
                last_ev = ev_sqev[-1]

                def tria(E, ext=ext, hl=hl):
                    return E.tensor_tensor(
                        out=sq[hl][:, ext - 128:ext],
                        in0=sq[hl][:, ext - 128:ext],
                        in1=trib[:], op=ALU.add)
                P.add("vector", tria, waits=[last_ev])
                ev_tria[hl] = P.cnt["dv"]

            th_ready = {}
            for hl in (1, 3):
                P.add("scalar", lambda E: E.memzero(thr2[:, 0:1]),
                      waits=[("dv", ev_tria[hl])])
                for it in range(NITER):
                    def cnta(E, ext=ext, hl=hl):
                        return E.activation(
                            out=junkb[:, 0:ext], in_=sq[hl][:, 0:ext],
                            func=AF.Sign, bias=thr2[:, 0:1], scale=1.0,
                            accum_out=thr2[:, 1:2])
                    P.add("scalar", cnta)

                    def sgna(E, g=g):
                        return E.activation(
                            out=thr2[:, 2:3], in_=thr2[:, 1:2],
                            func=AF.Sign, bias=tg2b[:, g:g + 1],
                            scale=-1.0)
                    P.add("scalar", sgna)

                    def mula(E, it=it):
                        return E.mul(thr2[:, 2:3], thr2[:, 2:3],
                                     steprb[:, it:it + 1])
                    P.add("scalar", mula)

                    def adda(E):
                        return E.add(thr2[:, 0:1], thr2[:, 0:1],
                                     thr2[:, 2:3])
                    P.add("scalar", adda)

                def tfina(E):
                    return E.add(thr2[:, 0:1], thr2[:, 0:1],
                                 steprb[:, NITER:NITER + 1])
                P.add("scalar", tfina)

                def tforcea(E, g=g):
                    return E.mul(thr2[:, 0:1], thr2[:, 0:1],
                                 fmulb[:, g:g + 1])
                P.add("scalar", tforcea)

                def tforceb(E, g=g):
                    return E.add(thr2[:, 0:1], thr2[:, 0:1],
                                 faddnb[:, g:g + 1])
                P.add("scalar", tforceb)

                def nega(E, hl=hl):
                    return E.mul(thr2[:, 3:4] if hl == 1 else thr2[:, 2:3],
                                 thr2[:, 0:1], cstb[:, 2:3])
                P.add("scalar", nega)
                th_ready[hl] = ("ac", P.cnt["ac"],
                                thr2[:, 3:4] if hl == 1 else thr2[:, 2:3])
            for hl in (0, 2):
                P.add("vector", lambda E: E.memset(thr[:, 0:1], 0.0),
                      waits=[("dv", ev_tria[hl])])
                for it in range(NITER):
                    step = B0 * (0.5 ** it)

                    def cnt_(E, ext=ext, hl=hl):
                        return E.tensor_scalar(
                            out=junkb2[:, 0:ext], in0=sq[hl][:, 0:ext],
                            scalar1=thr[:, 0:1], scalar2=None,
                            op0=ALU.is_ge, op1=ALU.add,
                            accum_out=thr[:, 1:2])
                    P.add("vector", cnt_)

                    def dlt(E, g=g, step=step):
                        return E.tensor_scalar(
                            out=thr[:, 2:3], in0=thr[:, 1:2],
                            scalar1=tgtb[:, g:g + 1], scalar2=2.0 * step,
                            op0=ALU.is_ge, op1=ALU.mult)
                    P.add("vector", dlt)

                    def upd(E, step=step):
                        return E.scalar_tensor_tensor(
                            out=thr[:, 0:1], in0=thr[:, 2:3],
                            scalar=-step, in1=thr[:, 0:1],
                            op0=ALU.add, op1=ALU.add)
                    P.add("vector", upd)
                last_step = B0 * (0.5 ** (NITER - 1))

                def tfin(E, last_step=last_step):
                    return E.tensor_scalar(
                        out=thr[:, 0:1], in0=thr[:, 0:1],
                        scalar1=-3.0 * last_step, scalar2=None,
                        op0=ALU.add)
                P.add("vector", tfin)

                def tforce(E, g=g):
                    return E.scalar_tensor_tensor(
                        out=thr[:, 0:1], in0=thr[:, 0:1],
                        scalar=fmulb[:, g:g + 1], in1=faddb[:, g:g + 1],
                        op0=ALU.mult, op1=ALU.add)
                P.add("vector", tforce)

                def tsave(E, hl=hl):
                    return E.tensor_copy(
                        out=thr[:, 3:4] if hl == 0 else thr2b[:, 0:1],
                        in_=thr[:, 0:1])
                P.add("vector", tsave)
                th_ready[hl] = ("dv", P.cnt["dv"],
                                thr[:, 3:4] if hl == 0 else thr2b[:, 0:1])
            prev_cnt_dv = P.cnt["dv"]
            prev_cnt_ac = P.cnt["ac"]

            last_tev = []
            for hl in range(4):
                sem, val, col = th_ready[hl]

                def ttr(E, col=col):
                    return E.transpose(psTR[0:1, 0:128], col, idfb[:])
                twl = [(sem, val)] + last_tev
                if hl == 0:
                    twl.append(("dv", last_tmat_dv))
                P.add("tensor", ttr, waits=twl)
                ttr_pe = P.cnt["pe"]

                def tev(E, hl=hl):
                    return E.activation(
                        out=trow[0:1, hl * 128:(hl + 1) * 128],
                        in_=psTR[0:1, 0:128], func=AF.Copy)
                P.add("scalar", tev, waits=[("pe", ttr_pe)])
                last_tev = [("ac", P.cnt["ac"])]
            trow_ac = P.cnt["ac"]

        if layer == 1:
            def mmtm(E):
                return E.matmul(psTR[:], onf[:], trow[:],
                                start=True, stop=True)
            P.add("tensor", mmtm, waits=[("ac", trow_ac),
                                         ("dv", last_tmat_dv)])
            tm_pe = P.cnt["pe"]

            def tmev(E):
                return E.tensor_copy(out=tmatb[:], in_=psTR[:])
            P.add("vector", tmev, waits=[("pe", tm_pe)])
            last_tmat_dv = P.cnt["dv"]

        pend = []

        def emit_pvz(p, par, mypgi, u=u, nkc=nkc):
            pv_wl = [ev_pP[mypgi]]
            if p == 0:
                pv_wl.append(("ac", last_oevac))

            def mmpv(E, u=u, p=p, par=par, nkc=nkc):
                return E.matmul(
                    psO[:], vb[:, p * 256 + u * 128:p * 256 + (u + 1) * 128],
                    pbuf[par][:], start=(p == 0), stop=(p == nkc - 1))
            P.add("tensor", mmpv, waits=pv_wl)

            z_wl = []
            if p == 0:
                z_wl.append(("dv", last_zevac))

            def mmz(E, par=par, p=p, nkc=nkc):
                return E.matmul(
                    psZ[0:1, :], on1[:], pbuf[par][:],
                    start=(p == 0), stop=(p == nkc - 1))
            P.add("tensor", mmz, waits=z_wl)
            ev_pcons[mypgi] = P.cnt["pe"]

        for p in range(nkc):
            par = pgi % 3
            wl = []
            if p == 0:
                wl += list(first_wait)
            if pgi >= 3:
                wl.extend(ev_expc[pgi - 3])

            def mmst(E, j=j, u=u, p=p, par=par):
                return E.matmul(
                    psST[par][:], kb[:, u * S + p * 128:u * S + (p + 1) * 128],
                    qt[j % 2][:], start=True,
                    stop=True)
            P.add("tensor", mmst, waits=wl)
            mm_pe = P.cnt["pe"]

            exp_wl = [("pe", mm_pe)]
            if p == nkc - 1:
                def triST(E, par=par):
                    return E.tensor_tensor(
                        out=psST[par][:].rearrange("p (h q) -> p h q", q=128),
                        in0=psST[par][:].rearrange("p (h q) -> p h q", q=128),
                        in1=tritb[:].unsqueeze(1).to_broadcast([128, 4, 128]),
                        op=ALU.add)
                P.add("vector", triST, waits=[("pe", mm_pe)])
                exp_wl = [("dv", P.cnt["dv"])]

            if layer == 0:
                if pgi >= 3:
                    exp_wl.append(("pe", ev_pcons[pgi - 3]))

                def expP(E, par=par):
                    return E.activation(
                        out=pbuf[par][:], in_=psST[par][:], func=AF.Exp,
                        bias=cstb[:, 0:1], scale=cstb[:, 1:2])
                P.add("scalar", expP, waits=exp_wl)
                ev_pP[pgi] = ("ac", P.cnt["ac"])
                ev_expc.append([("ac", P.cnt["ac"])])
            else:
                if pgi >= 3:
                    exp_wl.append(ev_pP[pgi - 3])

                def expE(E, par=par):
                    return E.activation(
                        out=ebuf[par][:], in_=psST[par][:], func=AF.Exp,
                        bias=cstb[:, 0:1], scale=cstb[:, 1:2])
                P.add("scalar", expE, waits=exp_wl)
                e_ac = P.cnt["ac"]

                def mOp(E, par=par):
                    return E.tensor_tensor(
                        out=mb[:], in0=psST[par][:], in1=tmatb[:],
                        op=ALU.is_ge)
                P.add("vector", mOp, waits=[("ac", e_ac)])
                m_dv = P.cnt["dv"]

                pw = []
                if pgi >= 3:
                    pw.append(("pe", ev_pcons[pgi - 3]))

                def POp(E, par=par):
                    return E.tensor_tensor(
                        out=pbuf[par][:], in0=mb[:], in1=ebuf[par][:],
                        op=ALU.mult)
                P.add("vector", POp, waits=pw)
                ev_pP[pgi] = ("dv", P.cnt["dv"])
                ev_expc.append([("dv", P.cnt["dv"])])

            pend.append((p, par, pgi))
            if len(pend) > 2:
                emit_pvz(*pend.pop(0))
            pgi += 1
        while pend:
            emit_pvz(*pend.pop(0))
        pv_done = P.cnt["pe"]

        def zev(E):
            return E.tensor_scalar(
                out=zrow[:], in0=psZ[0:1, :], scalar1=1.0, scalar2=None,
                op0=ALU.mult)
        P.add("vector", zev, waits=[("pe", pv_done)])
        last_zevac = P.cnt["dv"]
        P.add("vector", lambda E: E.reciprocal(out=zir[:], in_=zrow[:]))
        zir_dv = P.cnt["dv"]

        def bcz(E):
            return E.matmul(psIZ[:], onf[:], zir[:], start=True, stop=True)
        P.add("tensor", bcz, waits=[("dv", zir_dv),
                                    ("dv", last_izmul)])
        bcz_pe = P.cnt["pe"]

        def oev(E):
            return E.activation(out=osb[:], in_=psO[:], func=AF.Copy)
        P.add("scalar", oev, waits=[("pe", pv_done), ("dv", P.cnt["dv"])])
        last_oevac = P.cnt["ac"]

        def omul(E):
            return E.tensor_tensor(out=onrm[:], in0=osb[:], in1=psIZ[:],
                                   op=ALU.mult)
        P.add("vector", omul, waits=[("ac", last_oevac), ("pe", bcz_pe),
                                     ("st", last_store)])
        last_izmul = P.cnt["dv"]
        om_dv = P.cnt["dv"]

        cquad = g // 8
        coff = (g % 8) * 128
        for i in range(4):
            mwl = [("dv", om_dv)]
            if i >= 2:
                mwl.append(("st", last_store))

            def msk(E, i=i, par=i % 2):
                return E.tensor_scalar(
                    out=osb2[par][:], in0=onrm[:],
                    scalar1=mvb[:, i:i + 1], scalar2=None, op0=ALU.mult)
            P.add("vector", msk, waits=mwl)
            msk_dv = P.cnt["dv"]

            def ost(E, u=u, i=i, cquad=cquad, coff=coff, par=i % 2):
                return E.dma_start(
                    out=RSin.rearrange("(z p) t -> p z t", p=128)
                    [:, cquad * 32 + i * 8 + u * 4:cquad * 32 + i * 8 + u * 4 + 4,
                     coff:coff + 128],
                    in_=osb2[par][:].rearrange("p (h q) -> p h q", q=128))
            P.add("scalar", ost, waits=[("dv", msk_dv)], inc="st", inck=16)
            last_store = P.cnt["st"]

        if j + 2 < len(jobs):
            plan_qload(j + 2, [("pe", P.cnt["pe"])])

    rsop(P, "ccot0" if layer == 0 else "ccot1", RSin, oTm, GROUPS4,
         [("st", P.cnt["st"])])



def sec_C12(nc, P, e, T):
    NT = 512
    rbuf = e(nc.sbuf_tensor("crb", [128, 32 * NT], BF16))
    x1buf = e(nc.sbuf_tensor("cx1", [128, 32 * NT], BF16))
    wtile = [e(nc.sbuf_tensor(f"cwt{i}", [128, 32 * 128], BF16)) for i in range(2)]
    wbig = [e(nc.sbuf_tensor(f"cwb{i}", [128, 32 * 512], BF16)) for i in range(2)]
    x0c = [e(nc.sbuf_tensor(f"cx0{i}", [128, 512], F32)) for i in range(2)]
    bqb = e(nc.sbuf_tensor("cbq", [128, D], BF16))
    wrb = e(nc.sbuf_tensor("cwr", [128, 32 * 8], BF16))
    brb = e(nc.sbuf_tensor("cbr", [1, 8], BF16))
    bbt = e(nc.sbuf_tensor("cbb", [128, 32], F32))
    io8 = e(nc.sbuf_tensor("cio", [128, 8], F32))
    on8 = e(nc.sbuf_tensor("con8", [128, 8], F32))
    on1 = e(nc.sbuf_tensor("con1", [128, 1], BF16))
    onr = e(nc.sbuf_tensor("conr", [1, 128], BF16))
    onf = e(nc.sbuf_tensor("conf", [1, 512], F32))
    epsb = e(nc.sbuf_tensor("ceps", [1, 1], F32))
    idb = e(nc.sbuf_tensor("cid", [128, 128], BF16))
    r2t = e(nc.sbuf_tensor("cr2", [128, 512], BF16))
    tn = e(nc.sbuf_tensor("ctn", [128, 512], F32))
    xf = e(nc.sbuf_tensor("cxf", [128, 512], F32))
    rowa = e(nc.sbuf_tensor("crwa", [1, 512], F32))
    rowb = e(nc.sbuf_tensor("crwb", [1, 512], F32))
    rowc = e(nc.sbuf_tensor("crwc", [1, 512], F32))
    lgt = e(nc.sbuf_tensor("clg", [128, 8], F32))
    srt = e(nc.sbuf_tensor("csr", [128, 8], F32))
    cum = e(nc.sbuf_tensor("ccu", [128, 8], F32))
    tmp8 = e(nc.sbuf_tensor("ct8", [128, 8], F32))
    w8 = e(nc.sbuf_tensor("cw8", [128, 8], F32))
    sc1 = e(nc.sbuf_tensor("csc", [128, 4], F32))
    hwb = e(nc.sbuf_tensor("chw", [128, 128], BF16))
    hwT = e(nc.sbuf_tensor("chT", [128, 512], BF16))

    psA = [e(nc.psum_tensor(f"cpA{i}", [128, 512], F32)) for i in range(2)]
    psS1 = e(nc.psum_tensor("cpS1", [128, 512], F32))
    psS2 = e(nc.psum_tensor("cpS2", [128, 512], F32))
    psRt = e(nc.psum_tensor("cpRt", [128, 512], F32))
    psH = e(nc.psum_tensor("cpH", [128, 512], F32))
    psT = e(nc.psum_tensor("cpT", [128, 1024], BF16))

    ld0 = P.cnt["ld"]
    P.add("sync", lambda E: E.dma_start(out=io8[:], in_=T["iota8"]), inc="ld", inck=16)
    P.add("sync", lambda E: E.dma_start(out=on8[:], in_=T["ones8"]), inc="ld", inck=16)
    P.add("sync", lambda E: E.dma_start(out=on1[:], in_=T["onesb"]), inc="ld", inck=16)
    P.add("sync", lambda E: E.dma_start(out=onr[:], in_=T["onesr"]), inc="ld", inck=16)
    P.add("sync", lambda E: E.dma_start(out=onf[:], in_=T["onesf"]), inc="ld", inck=16)
    P.add("sync", lambda E: E.dma_start(out=epsb[:], in_=T["epsr"]), inc="ld", inck=16)
    P.add("sync", lambda E: E.dma_start(out=idb[:], in_=T["idn"]), inc="ld", inck=16)
    P.add("sync", lambda E: E.dma_start(out=bbt[:], in_=T["bbr"]), inc="ld", inck=16)
    ldg0 = P.cnt["ldg"]
    P.add("gpsimd", lambda E: E.dma_start(out=bqb[:], in_=T["Bq"]), inc="ldg", inck=16)
    P.add("gpsimd", lambda E: E.dma_start(
        out=wrb[:].rearrange("p (kc n) -> p kc n", n=8),
        in_=T["Wr"].rearrange("(kc p) n -> p kc n", p=128)), inc="ldg", inck=16)
    P.add("gpsimd", lambda E: E.dma_start(out=brb[:], in_=T["brr"]), inc="ldg", inck=16)
    LD_ALL = P.cnt["ld"]
    LDG_ALL = P.cnt["ldg"]

    sw_i = [0]

    def stream_wtile(src, col0, ncols, reuse_pe):
        par = sw_i[0] % 2
        sem = f"lwa{par}"

        def ldw(E, src=src, col0=col0, ncols=ncols, par=par):
            return E.dma_start(
                out=wtile[par][:, 0:32 * ncols],
                in_=src[:, col0 * 32:(col0 + ncols) * 32])
        P.add("gpsimd", ldw, waits=[("pe", reuse_pe)], inc=sem, inck=16)
        sw_i[0] += 1
        return (sem, P.cnt[sem]), par

    def ln_rows():
        P.add("vector", lambda E: E.tensor_scalar(
            out=rowa[:, 0:NT], in0=psS1[0:1, 0:NT], scalar1=1.0 / D,
            scalar2=None, op0=ALU.mult), waits=[("pe", P.cnt["pe"])])
        mu_dv = P.cnt["dv"]
        P.add("vector", lambda E: E.tensor_scalar(
            out=rowb[:, 0:NT], in0=psS2[0:1, 0:NT], scalar1=1.0 / D,
            scalar2=None, op0=ALU.mult))
        P.add("scalar", lambda E: E.activation(
            out=rowc[:, 0:NT], in_=rowa[:, 0:NT], func=AF.Square),
            waits=[("dv", mu_dv)])
        sq_ac = P.cnt["ac"]
        P.add("vector", lambda E: E.tensor_tensor(
            out=rowb[:, 0:NT], in0=rowb[:, 0:NT], in1=rowc[:, 0:NT],
            op=ALU.subtract), waits=[("ac", sq_ac)])
        var_dv = P.cnt["dv"]
        P.add("scalar", lambda E: E.activation(
            out=rowc[:, 0:NT], in_=rowb[:, 0:NT], func=AF.Ln,
            bias=epsb[0:1, 0:1], scale=1.0), waits=[("dv", var_dv)])
        P.add("scalar", lambda E: E.activation(
            out=rowc[:, 0:NT], in_=rowc[:, 0:NT], func=AF.Exp,
            bias=0.0, scale=-0.5))
        return P.cnt["ac"]

    def bcast_rows(rstd_ac):
        P.add("tensor", lambda E: E.matmul(
            psA[0][:, 0:NT], onf[0:1, 0:128], rowa[0:1, 0:NT],
            start=True, stop=True), waits=[("ac", rstd_ac)])
        P.add("tensor", lambda E: E.matmul(
            psA[1][:, 0:NT], onf[0:1, 0:128], rowc[0:1, 0:NT],
            start=True, stop=True))
        return P.cnt["pe"]

    def matvec_block(wpar, rhs_ap_fn, doc, extra=(), close=True):
        for kc in range(32):
            def mmw(E, kc=kc, doc=doc, wpar=wpar, close=close):
                return E.matmul(
                    psA[doc % 2][:], wtile[wpar][:, kc * 128:(kc + 1) * 128],
                    rhs_ap_fn(kc), start=(kc == 0),
                    stop=(close and kc == 31))
            P.add("tensor", mmw, waits=(list(extra) if kc == 0 else []))
        return P.cnt["pe"]

    def ln_block(src_buf, dst_buf, store_f32, store_bf, tc):
        rstd_ac = ln_rows()
        bc_pe = bcast_rows(rstd_ac)
        for doc in range(32):
            def nrm1(E, doc=doc):
                return E.tensor_tensor(
                    out=tn[:], in0=src_buf[:, doc * NT:(doc + 1) * NT],
                    in1=psA[0][:], op=ALU.subtract)
            P.add("vector", nrm1, waits=([("pe", bc_pe)] if doc == 0 else []))

            def nrm2(E, doc=doc):
                return E.tensor_tensor(
                    out=dst_buf[:, doc * NT:(doc + 1) * NT], in0=tn[:],
                    in1=psA[1][:], op=ALU.mult)
            P.add("vector", nrm2)
            n2_dv = P.cnt["dv"]
            if store_f32 is not None:
                def cvt(E, doc=doc):
                    return E.activation(
                        out=xf[:], in_=dst_buf[:, doc * NT:(doc + 1) * NT],
                        func=AF.Copy)
                P.add("scalar", cvt, waits=[("dv", n2_dv), ("st", P.cnt["st"])])
                cvt_ac = P.cnt["ac"]

                def stx(E, doc=doc, tc=tc):
                    return E.dma_start(
                        out=store_f32[doc * 128:(doc + 1) * 128,
                                      tc * NT:(tc + 1) * NT], in_=xf[:])
                P.add("scalar", stx, waits=[("ac", cvt_ac)], inc="st", inck=16)
            if store_bf is not None:
                def stb(E, doc=doc, tc=tc):
                    return E.dma_start(
                        out=store_bf[doc * 128:(doc + 1) * 128,
                                     tc * NT:(tc + 1) * NT],
                        in_=dst_buf[:, doc * NT:(doc + 1) * NT])
                P.add("scalar", stb, waits=[("dv", n2_dv)], inc="st", inck=16)
        return P.cnt["dv"]

    x0_reuse = []
    for tc in range(2):
        tc0 = tc * NT
        def ldot(E, tc0=tc0):
            return E.dma_start(
                out=wbig[0][:].rearrange("p (kc n) -> p kc n", n=512),
                in_=T["oTm0"][:, tc0:tc0 + NT].rearrange(
                    "(kc p) n -> p kc n", p=128))
        lot_wl = [("pe", P.cnt["pe"])]
        if tc == 0:
            lot_wl.append(("ccot0", 1))
        P.add("sync", ldot, waits=lot_wl, inc="lot", inck=16)
        ot_ld = P.cnt["lot"]

        wt_cons = [P.cnt["pe"], P.cnt["pe"]]
        for doc in range(32):
            wld, wpar = stream_wtile(T["wo0"], doc * 128, 128,
                                     reuse_pe=wt_cons[doc % 2])

            def ldx0(E, doc=doc, tc0=tc0, par=doc % 2):
                return E.dma_start(
                    out=x0c[par][:],
                    in_=T["x0T"][doc * 128:(doc + 1) * 128, tc0:tc0 + NT])
            x0sem = "lx0a" if doc % 2 == 0 else "lx0b"
            P.add("sync", ldx0,
                  waits=([("dv", x0_reuse[-2])] if len(x0_reuse) >= 2 else []),
                  inc=x0sem, inck=16)
            x0_ld = (x0sem, P.cnt[x0sem])

            wl = [wld, ("lot", ot_ld)]
            if doc == 0 and tc == 0:
                wl.append(("ld", LD_ALL))
                wl.append(("ldg", LDG_ALL))
            if doc >= 2:
                wl.append(("dv", x0_reuse[-2]))
            else:
                wl.append(("dv", P.cnt["dv"]))
            mm_pe = matvec_block(
                wpar, lambda kc: wbig[0][:, kc * 512:(kc + 1) * 512], doc,
                extra=wl)
            wt_cons[doc % 2] = mm_pe

            def evr(E, doc=doc, par=doc % 2):
                return E.tensor_tensor(
                    out=rbuf[:, doc * NT:(doc + 1) * NT],
                    in0=psA[doc % 2][:], in1=x0c[par][:], op=ALU.add)
            P.add("vector", evr, waits=[("pe", mm_pe), x0_ld])
            x0_reuse.append(P.cnt["dv"])
            ev_dv = P.cnt["dv"]

            def sqr(E, doc=doc):
                return E.activation(
                    out=r2t[:], in_=rbuf[:, doc * NT:(doc + 1) * NT],
                    func=AF.Square)
            P.add("scalar", sqr,
                  waits=[("dv", ev_dv), ("pe", P.cnt["pe"])])
            sq_ac = P.cnt["ac"]

            def st1(E, doc=doc):
                return E.matmul(psS1[0:1, 0:NT], on1[:],
                                rbuf[:, doc * NT:(doc + 1) * NT],
                                start=(doc == 0), stop=(doc == 31))
            P.add("tensor", st1, waits=[("dv", ev_dv)])

            def st2(E, doc=doc):
                return E.matmul(psS2[0:1, 0:NT], on1[:], r2t[:],
                                start=(doc == 0), stop=(doc == 31))
            P.add("tensor", st2, waits=[("ac", sq_ac)])
        ln_block(rbuf, x1buf, None, None, tc)
        x1_dv = P.cnt["dv"]

        def ldaq(E):
            return E.dma_start(
                out=wbig[1][:, 0:32 * 128].rearrange("p (kc n) -> p kc n", n=128),
                in_=T["Aq"].rearrange("(kc p) n -> p kc n", p=128))
        P.add("gpsimd", ldaq, waits=[("pe", P.cnt["pe"])], inc="lwb1", inck=16)
        aq_ld = ("lwb1", P.cnt["lwb1"])
        for tt in range(4):
            for kc in range(32):
                def mmr(E, kc=kc, tt=tt):
                    return E.matmul(
                        psRt[:, 0:8],
                        x1buf[:, kc * NT + tt * 128:kc * NT + (tt + 1) * 128],
                        wrb[:, kc * 8:(kc + 1) * 8],
                        start=(kc == 0), stop=False)
                wl_r = []
                if tt == 0 and kc == 0:
                    wl_r = [("dv", x1_dv)]
                elif kc == 0:
                    wl_r = [("ac", P.cnt["ac"])]
                P.add("tensor", mmr, waits=wl_r)

            def mmrb(E):
                return E.matmul(psRt[:, 0:8], onr[:], brb[:],
                                start=False, stop=True)
            P.add("tensor", mmrb)
            rt_pe = P.cnt["pe"]

            def evl(E):
                return E.activation(out=lgt[:], in_=psRt[:, 0:8], func=AF.Copy)
            P.add("scalar", evl, waits=[("pe", rt_pe)])
            lgt_ac = P.cnt["ac"]

            P.add("vector", lambda E: E.max(out=srt[:], in_=lgt[:]),
                  waits=[("ac", lgt_ac)])
            P.add("vector", lambda E: E.tensor_tensor_scan(
                out=cum[:], data0=on8[:], data1=srt[:], initial=0.0,
                op0=ALU.mult, op1=ALU.add))
            P.add("vector", lambda E: E.tensor_tensor(
                out=tmp8[:], in0=srt[:], in1=io8[:], op=ALU.mult))
            P.add("vector", lambda E: E.tensor_scalar(
                out=tmp8[:], in0=tmp8[:], scalar1=1.0, scalar2=None,
                op0=ALU.add))
            P.add("vector", lambda E: E.tensor_tensor(
                out=tmp8[:], in0=tmp8[:], in1=cum[:], op=ALU.is_gt))
            P.add("vector", lambda E: E.tensor_scalar(
                out=w8[:], in0=tmp8[:], scalar1=0.0, scalar2=None,
                op0=ALU.add, op1=ALU.add, accum_out=sc1[:, 0:1]))
            P.add("vector", lambda E: E.tensor_tensor(
                out=cum[:], in0=srt[:], in1=tmp8[:], op=ALU.mult))
            P.add("vector", lambda E: E.tensor_scalar(
                out=w8[:], in0=cum[:], scalar1=0.0, scalar2=None,
                op0=ALU.add, op1=ALU.add, accum_out=sc1[:, 1:2]))
            P.add("vector", lambda E: E.reciprocal(
                out=sc1[:, 2:3], in_=sc1[:, 0:1]))
            P.add("vector", lambda E: E.tensor_scalar(
                out=sc1[:, 3:4], in0=sc1[:, 1:2], scalar1=1.0, scalar2=None,
                op0=ALU.subtract))
            P.add("vector", lambda E: E.tensor_tensor(
                out=sc1[:, 3:4], in0=sc1[:, 3:4], in1=sc1[:, 2:3],
                op=ALU.mult))
            P.add("vector", lambda E: E.tensor_scalar(
                out=w8[:], in0=lgt[:], scalar1=sc1[:, 3:4], scalar2=0.0,
                op0=ALU.subtract, op1=ALU.max))
            w8_dv = P.cnt["dv"]

            for kc in range(32):
                def mmh(E, kc=kc, tt=tt):
                    return E.matmul(
                        psH[:, 0:128],
                        x1buf[:, kc * NT + tt * 128:kc * NT + (tt + 1) * 128],
                        wbig[1][:, kc * 128:(kc + 1) * 128],
                        start=(kc == 0), stop=(kc == 31))
                P.add("tensor", mmh,
                      waits=([aq_ld] if kc == 0 else []))
            h_pe = P.cnt["pe"]

            def hwmul(E):
                return E.tensor_tensor(
                    out=hwb[:].rearrange("p (e r) -> p e r", r=16),
                    in0=psH[:, 0:128].rearrange("p (e r) -> p e r", r=16),
                    in1=w8[:].unsqueeze(2).to_broadcast([128, 8, 16]),
                    op=ALU.mult)
            P.add("vector", hwmul, waits=[("pe", h_pe), ("dv", w8_dv)])
            hw_dv = P.cnt["dv"]

            def htr(E):
                return E.transpose(psT[:, 0:128], hwb[:], idb[:])
            P.add("tensor", htr, waits=[("dv", hw_dv), ("ac", P.cnt["ac"])])
            htr_pe = P.cnt["pe"]

            def hev(E, tt=tt):
                return E.activation(out=hwT[:, tt * 128:(tt + 1) * 128],
                                    in_=psT[:, 0:128], func=AF.Copy)
            P.add("scalar", hev, waits=[("pe", htr_pe)])
        hwT_ac = P.cnt["ac"]

        ev2_list = []
        wt_cons2 = [P.cnt["pe"], P.cnt["pe"]]
        for doc in range(32):
            wld, wpar = stream_wtile(T["Wb"], doc * 128, 128,
                                     reuse_pe=wt_cons2[doc % 2])
            wl = [wld]
            if doc == 0:
                wl.append(("ac", hwT_ac))
            if doc >= 2:
                wl.append(("dv", ev2_list[-2]))
            else:
                wl.append(("dv", P.cnt["dv"]))
            mm_pe = matvec_block(
                wpar, lambda kc: x1buf[:, kc * NT:(kc + 1) * NT], doc,
                extra=wl, close=False)

            def mmexp(E, doc=doc):
                return E.matmul(
                    psA[doc % 2][:], bqb[:, doc * 128:(doc + 1) * 128],
                    hwT[:], start=False, stop=True)
            P.add("tensor", mmexp)
            mm_pe = P.cnt["pe"]
            wt_cons2[doc % 2] = mm_pe

            def evr2(E, doc=doc):
                return E.scalar_tensor_tensor(
                    out=rbuf[:, doc * NT:(doc + 1) * NT],
                    in0=psA[doc % 2][:], scalar=bbt[:, doc:doc + 1],
                    in1=x1buf[:, doc * NT:(doc + 1) * NT],
                    op0=ALU.add, op1=ALU.add)
            P.add("vector", evr2, waits=[("pe", mm_pe)])
            ev_dv = P.cnt["dv"]
            ev2_list.append(ev_dv)

            def sqr2(E, doc=doc):
                return E.activation(
                    out=r2t[:], in_=rbuf[:, doc * NT:(doc + 1) * NT],
                    func=AF.Square)
            P.add("scalar", sqr2,
                  waits=[("dv", ev_dv), ("pe", P.cnt["pe"])])
            sq_ac = P.cnt["ac"]

            def st1b(E, doc=doc):
                return E.matmul(psS1[0:1, 0:NT], on1[:],
                                rbuf[:, doc * NT:(doc + 1) * NT],
                                start=(doc == 0), stop=(doc == 31))
            P.add("tensor", st1b, waits=[("dv", ev_dv)])

            def st2b(E, doc=doc):
                return E.matmul(psS2[0:1, 0:NT], on1[:], r2t[:],
                                start=(doc == 0), stop=(doc == 31))
            P.add("tensor", st2b, waits=[("ac", sq_ac)])
        ln_block(rbuf, rbuf, T["x2T"], T["x2b"], tc)

    agop(P, "ccx2", T["x2b"], T["x2g"], GROUPS4, [("st", P.cnt["st"])])



def sec_E(nc, P, e, T):
    NT = 512
    NIC = 86
    DGRPS = [(0, 6), (6, 6), (12, 6), (18, 6), (24, 6), (30, 2)]

    x3buf = e(nc.sbuf_tensor("ex3", [128, 32 * NT], BF16))
    rbuf = e(nc.sbuf_tensor("erb", [128, 32 * NT], BF16))
    obuf = e(nc.sbuf_tensor("eob", [128, 32 * NT], BF16))
    wt4 = [e(nc.sbuf_tensor(f"ewt{i}", [128, 32 * 128], BF16)) for i in range(4)]
    wdbuf = [e(nc.sbuf_tensor(f"ewd{i}", [128, 6 * 128], BF16)) for i in range(2)]
    hbuf = [e(nc.sbuf_tensor(f"ehb{i}", [128, 512], BF16)) for i in range(2)]
    x2c = [e(nc.sbuf_tensor(f"ex2{i}", [128, 512], F32)) for i in range(2)]
    gsb = e(nc.sbuf_tensor("egs", [128, 512], BF16))
    htile = e(nc.sbuf_tensor("eht", [128, 512], BF16))
    r2t = e(nc.sbuf_tensor("er2", [128, 512], BF16))
    tn = e(nc.sbuf_tensor("etn", [128, 512], F32))
    xf = e(nc.sbuf_tensor("exf", [128, 512], F32))
    rowa = e(nc.sbuf_tensor("erwa", [1, 512], F32))
    rowb = e(nc.sbuf_tensor("erwb", [1, 512], F32))
    rowc = e(nc.sbuf_tensor("erwc", [1, 512], F32))
    on1 = e(nc.sbuf_tensor("eon1", [128, 1], BF16))
    onf = e(nc.sbuf_tensor("eonf", [1, 512], F32))
    epsb = e(nc.sbuf_tensor("eeps", [1, 1], F32))

    psD = [e(nc.psum_tensor(f"epD{i}", [128, 512], F32)) for i in range(6)]
    psS1 = e(nc.psum_tensor("epS1", [128, 512], F32))
    psS2 = e(nc.psum_tensor("epS2", [128, 512], F32))

    ld0 = P.cnt["ld"]
    P.add("sync", lambda E: E.dma_start(out=on1[:], in_=T["onesb"]), inc="ld", inck=16)
    P.add("sync", lambda E: E.dma_start(out=onf[:], in_=T["onesf"]), inc="ld", inck=16)
    P.add("sync", lambda E: E.dma_start(out=epsb[:], in_=T["epsr"]), inc="ld", inck=16)
    LD_ALL = P.cnt["ld"]

    def ln_rows():
        P.add("vector", lambda E: E.tensor_scalar(
            out=rowa[:, 0:NT], in0=psS1[0:1, 0:NT], scalar1=1.0 / D,
            scalar2=None, op0=ALU.mult), waits=[("pe", P.cnt["pe"])])
        mu_dv = P.cnt["dv"]
        P.add("vector", lambda E: E.tensor_scalar(
            out=rowb[:, 0:NT], in0=psS2[0:1, 0:NT], scalar1=1.0 / D,
            scalar2=None, op0=ALU.mult))
        P.add("scalar", lambda E: E.activation(
            out=rowc[:, 0:NT], in_=rowa[:, 0:NT], func=AF.Square),
            waits=[("dv", mu_dv)])
        sq_ac = P.cnt["ac"]
        P.add("vector", lambda E: E.tensor_tensor(
            out=rowb[:, 0:NT], in0=rowb[:, 0:NT], in1=rowc[:, 0:NT],
            op=ALU.subtract), waits=[("ac", sq_ac)])
        var_dv = P.cnt["dv"]
        P.add("scalar", lambda E: E.activation(
            out=rowc[:, 0:NT], in_=rowb[:, 0:NT], func=AF.Ln,
            bias=epsb[0:1, 0:1], scale=1.0), waits=[("dv", var_dv)])
        P.add("scalar", lambda E: E.activation(
            out=rowc[:, 0:NT], in_=rowc[:, 0:NT], func=AF.Exp,
            bias=0.0, scale=-0.5))
        return P.cnt["ac"]

    def bcast_rows(rstd_ac):
        P.add("tensor", lambda E: E.matmul(
            psD[0][:, 0:NT], onf[0:1, 0:128], rowa[0:1, 0:NT],
            start=True, stop=True), waits=[("ac", rstd_ac)])
        P.add("tensor", lambda E: E.matmul(
            psD[1][:, 0:NT], onf[0:1, 0:128], rowc[0:1, 0:NT],
            start=True, stop=True))
        return P.cnt["pe"]

    x2_reuse = []
    for tc in range(2):
        tc0 = tc * NT
        def ldot(E, tc0=tc0):
            return E.dma_start(
                out=obuf[:].rearrange("p (kc n) -> p kc n", n=512),
                in_=T["oTm1"][:, tc0:tc0 + NT].rearrange(
                    "(kc p) n -> p kc n", p=128))
        lot_wl = [("pe", P.cnt["pe"])]
        if tc == 0:
            lot_wl.append(("ccot1", 1))
        P.add("gpsimd", ldot, waits=lot_wl, inc="lot", inck=16)
        ot_ld = P.cnt["lot"]

        wo_cons = [P.cnt["pe"], P.cnt["pe"]]
        for doc in range(32):
            reuse = wo_cons[doc % 2]
            par = doc % 2
            wsem = f"lwa{par}"

            def ldw1(E, doc=doc, par=par):
                return E.dma_start(
                    out=wt4[par][:],
                    in_=T["wo1"][:, doc * 4096:(doc + 1) * 4096])
            P.add("gpsimd", ldw1, waits=[("pe", reuse)], inc=wsem, inck=16)
            wld = (wsem, P.cnt[wsem])

            def ldx2(E, doc=doc, tc0=tc0, par=par):
                return E.dma_start(
                    out=x2c[par][:],
                    in_=T["x2T"][doc * 128:(doc + 1) * 128, tc0:tc0 + NT])
            x2sem = "lx0a" if par == 0 else "lx0b"
            P.add("sync", ldx2,
                  waits=([("dv", x2_reuse[-2])] if len(x2_reuse) >= 2 else []),
                  inc=x2sem, inck=16)
            x2_ld = (x2sem, P.cnt[x2sem])

            wl = [wld, ("lot", ot_ld)]
            if doc == 0 and tc == 0:
                wl.append(("ld", LD_ALL))
            if doc >= 2:
                wl.append(("dv", x2_reuse[-2]))
            else:
                wl.append(("dv", P.cnt["dv"]))
                wl.append(("ac", P.cnt["ac"]))
            for kc in range(32):
                def mmwo(E, kc=kc, doc=doc, par=par):
                    return E.matmul(
                        psD[doc % 2][:], wt4[par][:, kc * 128:(kc + 1) * 128],
                        obuf[:, kc * 512:(kc + 1) * 512],
                        start=(kc == 0), stop=(kc == 31))
                P.add("tensor", mmwo, waits=(wl if kc == 0 else []))
            mm_pe = P.cnt["pe"]
            wo_cons[doc % 2] = mm_pe

            def evr(E, doc=doc, par=par):
                return E.tensor_tensor(
                    out=rbuf[:, doc * NT:(doc + 1) * NT],
                    in0=psD[doc % 2][:], in1=x2c[par][:], op=ALU.add)
            P.add("vector", evr, waits=[("pe", mm_pe), x2_ld])
            x2_reuse.append(P.cnt["dv"])
            ev_dv = P.cnt["dv"]

            def sqr(E, doc=doc):
                return E.activation(
                    out=r2t[:], in_=rbuf[:, doc * NT:(doc + 1) * NT],
                    func=AF.Square)
            P.add("scalar", sqr,
                  waits=[("dv", ev_dv), ("pe", P.cnt["pe"])])
            sq_ac = P.cnt["ac"]

            def st1(E, doc=doc):
                return E.matmul(psS1[0:1, 0:NT], on1[:],
                                rbuf[:, doc * NT:(doc + 1) * NT],
                                start=(doc == 0), stop=(doc == 31))
            P.add("tensor", st1, waits=[("dv", ev_dv)])

            def st2(E, doc=doc):
                return E.matmul(psS2[0:1, 0:NT], on1[:], r2t[:],
                                start=(doc == 0), stop=(doc == 31))
            P.add("tensor", st2, waits=[("ac", sq_ac)])

        rstd_ac = ln_rows()
        bc_pe = bcast_rows(rstd_ac)
        for doc in range(32):
            def nrm1(E, doc=doc):
                return E.tensor_tensor(
                    out=tn[:], in0=rbuf[:, doc * NT:(doc + 1) * NT],
                    in1=psD[0][:], op=ALU.subtract)
            P.add("vector", nrm1, waits=([("pe", bc_pe)] if doc == 0 else []))

            def nrm2(E, doc=doc):
                return E.tensor_tensor(
                    out=x3buf[:, doc * NT:(doc + 1) * NT], in0=tn[:],
                    in1=psD[1][:], op=ALU.mult)
            P.add("vector", nrm2)
        x3_dv = P.cnt["dv"]

        h_done = []
        si_list = []
        gu_cons = [P.cnt["pe"], P.cnt["pe"]]
        for ic in range(NIC):
            reuse = gu_cons[ic % 2]
            par = ic % 2
            cc_, lt = t2cl(ic)

            def ldgw(E, cc_=cc_, lt=lt, par=par):
                return E.dma_start(
                    out=wt4[par][:],
                    in_=T["slabgG"][cc_ * 128:(cc_ + 1) * 128,
                                    lt * 4096:(lt + 1) * 4096])
            wl_g = [("pe", reuse)]
            if ic == 0:
                wl_g.append(("ccsg", 1))
            P.add("sync", ldgw, waits=wl_g, inc=f"lwg{par}", inck=16)
            g_ld = (f"lwg{par}", P.cnt[f"lwg{par}"])

            def ldu(E, cc_=cc_, lt=lt, par=par):
                return E.dma_start(
                    out=wt4[2 + par][:],
                    in_=T["slabuG"][cc_ * 128:(cc_ + 1) * 128,
                                    lt * 4096:(lt + 1) * 4096])
            wl_u0 = [("pe", reuse)]
            if ic == 0:
                wl_u0.append(("ccsu", 1))
            P.add("sync", ldu, waits=wl_u0, inc=f"lwu{par}", inck=16)
            u_ld = (f"lwu{par}", P.cnt[f"lwu{par}"])

            wl = [g_ld]
            if ic == 0:
                wl.append(("dv", x3_dv))
                wl.append(("dv", P.cnt["dv"]))
                wl.append(("ac", P.cnt["ac"]))
            else:
                wl.append(("ac", si_list[-1]))
            for kc in range(32):
                def mmg(E, kc=kc, par=par):
                    return E.matmul(
                        psD[0][:], wt4[par][:, kc * 128:(kc + 1) * 128],
                        x3buf[:, kc * NT:(kc + 1) * NT],
                        start=(kc == 0), stop=(kc == 31))
                P.add("tensor", mmg, waits=(wl if kc == 0 else []))
            g_pe = P.cnt["pe"]
            for kc in range(32):
                def mmu(E, kc=kc, par=par):
                    return E.matmul(
                        psD[1][:], wt4[2 + par][:, kc * 128:(kc + 1) * 128],
                        x3buf[:, kc * NT:(kc + 1) * NT],
                        start=(kc == 0), stop=(kc == 31))
                wl_u = [u_ld]
                if h_done:
                    wl_u.append(("dv", h_done[-1]))
                else:
                    wl_u.append(("dv", P.cnt["dv"]))
                P.add("tensor", mmu, waits=(wl_u if kc == 0 else []))
            u_pe = P.cnt["pe"]
            gu_cons[ic % 2] = u_pe

            def silu(E):
                return E.activation(out=gsb[:], in_=psD[0][:], func=AF.Silu)
            P.add("scalar", silu, waits=[("pe", g_pe), ("sth", P.cnt["sth"])])
            si_ac = P.cnt["ac"]
            si_list.append(si_ac)

            def hmul(E):
                return E.tensor_tensor(out=htile[:], in0=gsb[:],
                                       in1=psD[1][:], op=ALU.mult)
            P.add("vector", hmul, waits=[("pe", u_pe), ("ac", si_ac),
                                         ("sth", P.cnt["sth"])])
            h_dv = P.cnt["dv"]
            h_done.append(h_dv)

            def sth_(E, ic=ic, tc0=tc0):
                return E.dma_start(
                    out=T["h_scr"][ic * 128:(ic + 1) * 128, tc0:tc0 + NT],
                    in_=htile[:])
            P.add("scalar", sth_, waits=[("dv", h_dv)], inc="sth", inck=16)
        h_all = P.cnt["sth"]

        for gi, (d0, nd) in enumerate(DGRPS):
            grp_dv = P.cnt["dv"]
            grp_ac = P.cnt["ac"]
            dh_cons = [P.cnt["pe"], P.cnt["pe"]]
            for ic in range(NIC):
                reuse = dh_cons[ic % 2]
                par = ic % 2
                cc_, lt = t2cl(ic)

                def ldwd(E, cc_=cc_, lt=lt, d0=d0, nd=nd, par=par):
                    return E.dma_start(
                        out=wdbuf[par][:, 0:nd * 128],
                        in_=T["slabdG"][cc_ * 128:(cc_ + 1) * 128,
                                        (lt * 32 + d0) * 128:
                                        (lt * 32 + d0 + nd) * 128])
                wl_d0 = [("pe", reuse)]
                if gi == 0 and ic == 0 and tc == 0:
                    wl_d0.append(("ccsd", 1))
                P.add("sync", ldwd, waits=wl_d0, inc=f"lwd{par}", inck=16)
                wd_ld = (f"lwd{par}", P.cnt[f"lwd{par}"])

                def ldh(E, ic=ic, tc0=tc0, par=par):
                    return E.dma_start(
                        out=hbuf[par][:],
                        in_=T["h_scr"][ic * 128:(ic + 1) * 128, tc0:tc0 + NT])
                P.add("sync", ldh, waits=[("sth", h_all), ("pe", reuse)],
                      inc=f"lh{par}", inck=16)
                h_ld = (f"lh{par}", P.cnt[f"lh{par}"])

                for d6 in range(nd):
                    def mmd(E, d6=d6, ic=ic, par=par):
                        return E.matmul(
                            psD[d6][:], wdbuf[par][:, d6 * 128:(d6 + 1) * 128],
                            hbuf[par][:],
                            start=(ic == 0), stop=(ic == NIC - 1))
                    wl_d = [wd_ld, h_ld]
                    if ic == 0:
                        wl_d += [("dv", grp_dv), ("ac", grp_ac)]
                    P.add("tensor", mmd,
                          waits=(wl_d if d6 == 0 else []))
                dh_cons[ic % 2] = P.cnt["pe"]
            down_pe = P.cnt["pe"]

            for d6 in range(nd):
                doc = d0 + d6
                ev_wl = [("pe", down_pe)]

                def evr4(E, doc=doc, d6=d6):
                    return E.tensor_tensor(
                        out=rbuf[:, doc * NT:(doc + 1) * NT],
                        in0=psD[d6][:],
                        in1=x3buf[:, doc * NT:(doc + 1) * NT], op=ALU.add)
                P.add("vector", evr4, waits=ev_wl)
                ev_dv = P.cnt["dv"]

                def sqr4(E, doc=doc):
                    return E.activation(
                        out=r2t[:], in_=rbuf[:, doc * NT:(doc + 1) * NT],
                        func=AF.Square)
                P.add("scalar", sqr4,
                      waits=[("dv", ev_dv), ("pe", P.cnt["pe"])])
                sq_ac = P.cnt["ac"]

                def st1d(E, doc=doc):
                    return E.matmul(psS1[0:1, 0:NT], on1[:],
                                    rbuf[:, doc * NT:(doc + 1) * NT],
                                    start=(doc == 0), stop=(doc == 31))
                P.add("tensor", st1d, waits=[("dv", ev_dv)])

                def st2d(E, doc=doc):
                    return E.matmul(psS2[0:1, 0:NT], on1[:], r2t[:],
                                    start=(doc == 0), stop=(doc == 31))
                P.add("tensor", st2d, waits=[("ac", sq_ac)])

        rstd_ac = ln_rows()
        bc_pe = bcast_rows(rstd_ac)
        for doc in range(32):
            def nrm1f(E, doc=doc):
                return E.tensor_tensor(
                    out=tn[:], in0=rbuf[:, doc * NT:(doc + 1) * NT],
                    in1=psD[0][:], op=ALU.subtract)
            P.add("vector", nrm1f, waits=([("pe", bc_pe)] if doc == 0 else []))

            def nrm2f(E, doc=doc):
                return E.tensor_tensor(
                    out=xf[:], in0=tn[:], in1=psD[1][:], op=ALU.mult)
            P.add("vector", nrm2f, waits=[("st", P.cnt["st"])])
            n2_dv = P.cnt["dv"]

            def stx(E, doc=doc, tc0=tc0):
                return E.dma_start(
                    out=T["xout"][doc * 128:(doc + 1) * 128, tc0:tc0 + NT],
                    in_=xf[:])
            P.add("scalar", stx, waits=[("dv", n2_dv)], inc="st", inck=16)

    P.ops["scalar"].append(
        (lambda E: E.wait_ge(P.sems["st"], P.cnt["st"]), [], None, 1))



SEMS = ("ld", "ldg", "lw0", "lw1", "lq0", "lq1", "lwa0", "lwa1", "lwb1",
        "lx0a", "lx0b", "lot", "lwg0", "lwg1", "lwu0", "lwu1",
        "lwd0", "lwd1", "lh0", "lh1", "sth", "vz", "zw", "gb", "vd", "ws",
        "pe", "ac", "dv", "st",
        "ccsg", "ccsu", "ccsd", "ccot0", "ccx2", "ccot1")


DBG = False


def build_fused():
    nc = bass.Bass("TRN2", target_bir_lowering=False, debug=False)
    T = {}

    def inp(name, shape, dt):
        T[name] = nc.dram_tensor(name, shape, dt, kind="ExternalInput").ap()

    def intl(name, shape, dt):
        T[name] = nc.dram_tensor(name, shape, dt, kind="Internal").ap()

    def intls(name, shape, dt):
        T[name] = nc.dram_tensor(name, shape, dt, kind="Internal",
                                 addr_space="Shared").ap()

    inp("xTb", [D, S], F32)
    inp("x0T", [D, TOK], F32)
    inp("wqkv0", [D, 1536], F32)
    inp("wqkv1", [D, 1536], F32)
    inp("cos0", [S, 64], F32)
    inp("sin0", [S, 64], F32)
    inp("cos1", [S, HD], F32)
    inp("sin1", [S, HD], F32)
    inp("idn", [128, 128], BF16)
    inp("gq", [D, 176], I32)
    inp("gz", [NG_GU, 176], I32)
    inp("gs", [NG_GU, 1408], F32)
    inp("uq", [D, 176], I32)
    inp("uz", [NG_GU, 176], I32)
    inp("us", [NG_GU, 1408], F32)
    inp("dqw", [NG_D * 128, 512], I32)
    inp("dz", [NG_D, 512], I32)
    inp("ds", [NG_D, D], F32)
    inp("tri", [128, 128], F32)
    inp("trit", [128, 128], F32)
    inp("cst", [128, 8], F32)
    inp("onesc", [128, 1], BF16)
    inp("idf", [128, 128], F32)
    inp("onfr", [1, 128], F32)
    inp("mv", [128, 4], F32)
    inp("tgt", [128, 32], F32)
    inp("tg2", [128, 32], F32)
    inp("fmul", [128, 32], F32)
    inp("fadd", [128, 32], F32)
    inp("faddn", [128, 32], F32)
    inp("stepr", [128, 24], F32)
    inp("wo0", [128, D * D // 128], F32)
    inp("Wb", [128, D * D // 128], F32)
    inp("bbr", [128, 32], F32)
    inp("Wr", [D, 8], F32)
    inp("brr", [1, 8], F32)
    inp("Aq", [D, 128], F32)
    inp("Bq", [128, D], F32)
    inp("iota8", [128, 8], F32)
    inp("ones8", [128, 8], F32)
    inp("onesb", [128, 1], BF16)
    inp("onesr", [1, 128], BF16)
    inp("onesf", [1, 512], F32)
    inp("epsr", [1, 1], F32)
    inp("wo1", [128, D * D // 128], F32)

    def intd(name, shape, dt):
        kind = "ExternalOutput" if DBG else "Internal"
        T[name] = nc.dram_tensor(name, shape, dt, kind=kind).ap()

    intd("q0T", [128, 8 * S], BF16)
    intd("k0T", [128, 2 * S], BF16)
    intd("v0", [S, 256], BF16)
    intd("q1T", [128, 8 * S], BF16)
    intd("k1T", [128, 2 * S], BF16)
    intd("v1", [S, 256], BF16)
    intl("slabg", [128, NG_D * 32 * 128], BF16)
    intl("slabu", [128, NG_D * 32 * 128], BF16)
    intl("slabd", [128, NG_D * 32 * 128], BF16)
    intls("slabgG", [128 * NCORES, NG_D * 32 * 128], BF16)
    intls("slabuG", [128 * NCORES, NG_D * 32 * 128], BF16)
    intls("slabdG", [128 * NCORES, NG_D * 32 * 128], BF16)
    intl("zscr", [2 * NG_GU + NG_D, 4096], F32)
    intl("RSin0", [4 * D, TOK], BF16)
    intl("RSin1", [4 * D, TOK], BF16)
    intl("oTm0", [D, TOK], BF16)
    intl("oTm1", [D, TOK], BF16)
    intd("x2T", [D, TOK], F32)
    intl("x2b", [D, TOK], BF16)
    intl("x2g", [4 * D, TOK], BF16)
    intl("h_scr", [NIC128 := 86 * 128, TOK], BF16)
    T["xout"] = nc.dram_tensor("xout", [D, TOK], F32, kind="ExternalOutput").ap()
    if DBG:
        for dn, shp, dt in (("dbg_oTm0", [D, TOK], BF16),
                            ("dbg_oTm1", [D, TOK], BF16),
                            ("dbg_slabg", [128, NG_D * 32 * 128], BF16),
                            ("dbg_slabd", [128, NG_D * 32 * 128], BF16)):
            T[dn] = nc.dram_tensor(dn, shp, dt, kind="ExternalOutput").ap()

    with ExitStack() as octx:
        P = Plan(nc)
        for sname in SEMS:
            P.sem(octx, sname)
        block = octx.enter_context(nc.Block())

        with ExitStack() as s:
            sec_proj(nc, P, s.enter_context, T, 0)
            sec_dequant(nc, P, s.enter_context, T)
            P.flush(block)
        P.barrier()
        with ExitStack() as s:
            sec_attn(nc, P, s.enter_context, T, 0)
            P.flush(block)
        if DBG:
            P.add("gpsimd", lambda E: E.dma_start(out=T["dbg_oTm0"],
                                                  in_=T["oTm0"]),
                  waits=[("ccot0", 1)], inc="st", inck=16)
            P.add("gpsimd", lambda E: E.dma_start(out=T["dbg_slabg"],
                                                  in_=T["slabg"]),
                  waits=[("ws", P.cnt["ws"])], inc="st", inck=16)
            P.add("gpsimd", lambda E: E.dma_start(out=T["dbg_slabd"],
                                                  in_=T["slabd"]),
                  waits=[("ws", P.cnt["ws"])], inc="st", inck=16)
        P.barrier()
        with ExitStack() as s:
            sec_C12(nc, P, s.enter_context, T)
            P.flush(block)
        P.barrier()
        with ExitStack() as s:
            sec_proj(nc, P, s.enter_context, T, 1)
            P.flush(block)
        P.barrier()
        ws_all = P.cnt["ws"]
        agop(P, "ccsg", T["slabg"], T["slabgG"], GROUPS8, [("ws", ws_all)])
        agop(P, "ccsu", T["slabu"], T["slabuG"], GROUPS8, [("ws", ws_all)])
        agop(P, "ccsd", T["slabd"], T["slabdG"], GROUPS8, [("ws", ws_all)])
        with ExitStack() as s:
            sec_attn(nc, P, s.enter_context, T, 1)
            P.flush(block)
        if DBG:
            P.add("gpsimd", lambda E: E.dma_start(out=T["dbg_oTm1"],
                                                  in_=T["oTm1"]),
                  waits=[("ccot1", 1)], inc="st", inck=16)
        P.barrier()
        with ExitStack() as s:
            sec_E(nc, P, s.enter_context, T)
            P.flush(block)
        P.check()
    return nc



def rope0_tables():
    freqs = 1.0 / (10000.0 ** (np.arange(0, HD, 2, dtype=np.float32) / HD))
    t = np.arange(S, dtype=np.float32)
    ang = np.outer(t, freqs)
    return np.cos(ang).astype(np.float32), np.sin(ang).astype(np.float32)


def rope1_tables():
    inv = 1.0 / (10000.0 ** (np.arange(0, HD, 2, dtype=np.float32) / HD))
    t = np.arange(S, dtype=np.float32)
    fr = np.outer(t, inv)
    emb = np.concatenate([fr, fr], axis=1)
    return np.cos(emb).astype(np.float32), np.sin(emb).astype(np.float32)


def hosttile(w, cc):
    K, N = w.shape
    return np.ascontiguousarray(
        w.reshape(K // 128, 128, N // cc, cc).transpose(1, 2, 0, 3)
        .reshape(128, (N // cc) * (K // 128) * cc))


def fused_inputs(inputs):
    x = np.asarray(inputs["L0_attn_x"], np.float32).reshape(B * S, D)
    wq = np.asarray(inputs["L0_attn_wq"], np.float32)
    wk = np.asarray(inputs["L0_attn_wk"], np.float32)
    wv = np.asarray(inputs["L0_attn_wv"], np.float32)
    cos0, sin0 = rope0_tables()
    cos1, sin1 = rope1_tables()
    wq1 = np.asarray(inputs["L1_attn_Wq"], np.float32).T
    wk1 = np.asarray(inputs["L1_attn_Wk"], np.float32).T
    wv1 = np.asarray(inputs["L1_attn_Wv"], np.float32).T

    gq = np.asarray(inputs["L1_ffn_gate_qweight"], np.int32)
    gz = np.asarray(inputs["L1_ffn_gate_qzeros"], np.int32)
    gs = np.asarray(inputs["L1_ffn_gate_scales"], np.float32)
    uq = np.asarray(inputs["L1_ffn_up_qweight"], np.int32)
    uz = np.asarray(inputs["L1_ffn_up_qzeros"], np.int32)
    us = np.asarray(inputs["L1_ffn_up_scales"], np.float32)
    dq = np.asarray(inputs["L1_ffn_down_qweight"], np.int32)
    dz = np.asarray(inputs["L1_ffn_down_qzeros"], np.int32)
    dsc = np.asarray(inputs["L1_ffn_down_scales"], np.float32)

    cst = np.zeros((128, 8), np.float32)
    cst[:, 0] = -20.0
    cst[:, 1] = 1.0 / math.sqrt(HD)
    cst[:, 2] = -1.0
    idn = np.eye(128, dtype=BF)
    tri = np.triu(np.full((128, 128), -1e30, np.float32), k=1)
    trit = np.tril(np.full((128, 128), -1e30, np.float32), k=-1)
    onesc = np.ones((128, 1), BF)
    idf = np.eye(128, dtype=np.float32)
    onfr = np.ones((1, 128), np.float32)
    NITER = 14
    B0 = 16.0
    tgt = np.zeros((128, 32), np.float32)
    tg2 = np.zeros((128, 32), np.float32)
    fmul = np.zeros((128, 32), np.float32)
    fadd = np.zeros((128, 32), np.float32)
    for g in range(32):
        nval = g * 128 + np.arange(128) + 1
        ext = (g + 1) * 128
        tgt[:, g] = np.minimum(410, nval)
        tg2[:, g] = 2.0 * tgt[:, g] - ext
        keep_all = nval <= 410
        fmul[:, g] = np.where(keep_all, 0.0, 1.0)
        fadd[:, g] = np.where(keep_all, -64.0, 0.0)
    stepr = np.zeros((128, 24), np.float32)
    for it in range(NITER):
        stepr[:, it] = B0 * (0.5 ** it)
    stepr[:, NITER] = 3.0 * B0 * (0.5 ** (NITER - 1))

    wo0 = hosttile(np.asarray(inputs["L0_attn_wo"], np.float32), 128)
    Wbt = hosttile(np.ascontiguousarray(
        np.asarray(inputs["L0_ffn_W_base"], np.float32).T), 128)
    bb = np.asarray(inputs["L0_ffn_b_base"], np.float32)
    bbt = np.ascontiguousarray(bb.reshape(32, 128).T)
    Wrt = np.ascontiguousarray(np.asarray(inputs["L0_ffn_W_router"], np.float32).T)
    br = np.asarray(inputs["L0_ffn_b_router"], np.float32).reshape(1, 8)
    Aq = np.ascontiguousarray(np.asarray(inputs["L0_ffn_lora_A"], np.float32)
                              .transpose(1, 0, 2).reshape(D, 128))
    Bq = np.ascontiguousarray(np.asarray(inputs["L0_ffn_lora_B"], np.float32)
                              .reshape(128, D))
    iota = np.tile(np.arange(1, 9, dtype=np.float32)[None, :], (128, 1))
    ones8 = np.ones((128, 8), np.float32)
    onesb = np.ones((128, 1), BF)
    onesr = np.ones((1, 128), BF)
    onesf = np.ones((1, 512), np.float32)
    epsr = np.full((1, 1), 1e-5, np.float32)
    wo1 = hosttile(np.ascontiguousarray(
        np.asarray(inputs["L1_attn_Wo"], np.float32).T), 128)

    shared = dict(
        cos0=np.ascontiguousarray(cos0), sin0=np.ascontiguousarray(sin0),
        cos1=np.ascontiguousarray(cos1), sin1=np.ascontiguousarray(sin1),
        idn=idn, tri=tri, trit=trit, cst=cst, onesc=onesc, idf=idf, onfr=onfr,
        tgt=tgt, tg2=tg2, fmul=fmul, fadd=fadd, faddn=-fadd, stepr=stepr,
        wo0=wo0, Wb=Wbt, bbr=bbt, Wr=Wrt, brr=br, Aq=Aq, Bq=Bq,
        iota8=iota, ones8=ones8, onesb=onesb, onesr=onesr, onesf=onesf,
        epsr=epsr, wo1=wo1,
    )

    maps = []
    for c in range(NCORES):
        b = c // 4
        h0 = 8 * (c % 4)
        kv0 = 2 * (c % 4)
        t0 = c * TOK
        go = GOFF[c]
        gw = GW[c]
        gq_c = np.zeros((D, 176), np.int32)
        gq_c[:, 0:gw // 8] = gq[:, go // 8:go // 8 + gw // 8]
        uq_c = np.zeros((D, 176), np.int32)
        uq_c[:, 0:gw // 8] = uq[:, go // 8:go // 8 + gw // 8]
        gz_c = np.zeros((NG_GU, 176), np.int32)
        gz_c[:, 0:gw // 8] = gz[:, go // 8:go // 8 + gw // 8]
        uz_c = np.zeros((NG_GU, 176), np.int32)
        uz_c[:, 0:gw // 8] = uz[:, go // 8:go // 8 + gw // 8]
        gs_c = np.zeros((NG_GU, 1408), np.float32)
        gs_c[:, 0:gw] = gs[:, go:go + gw]
        us_c = np.zeros((NG_GU, 1408), np.float32)
        us_c[:, 0:gw] = us[:, go:go + gw]
        dq_c = np.zeros((NG_D * 128, 512), np.int32)
        dq_c[0:gw] = dq[go:go + gw]
        dz_c = np.zeros((NG_D, 512), np.int32)
        dz_c[0:gw // 128] = dz[go // 128:go // 128 + gw // 128]
        ds_c = np.zeros((NG_D, D), np.float32)
        ds_c[0:gw // 128] = dsc[go // 128:go // 128 + gw // 128]
        mvv = np.zeros((128, 4), np.float32)
        mvv[:, c % 4] = 1.0
        m = dict(shared)
        m.update(
            xTb=np.ascontiguousarray(x[b * S:(b + 1) * S].T),
            x0T=np.ascontiguousarray(x[t0:t0 + TOK].T),
            wqkv0=np.ascontiguousarray(np.concatenate(
                [wq[:, h0 * HD:(h0 + 8) * HD],
                 wk[:, kv0 * HD:(kv0 + 2) * HD],
                 wv[:, kv0 * HD:(kv0 + 2) * HD]], axis=1)),
            wqkv1=np.ascontiguousarray(np.concatenate(
                [wq1[:, h0 * HD:(h0 + 8) * HD],
                 wk1[:, kv0 * HD:(kv0 + 2) * HD],
                 wv1[:, kv0 * HD:(kv0 + 2) * HD]], axis=1)),
            gq=gq_c, gz=gz_c, gs=gs_c, uq=uq_c, uz=uz_c, us=us_c,
            dqw=dq_c, dz=dz_c, ds=ds_c, mv=mvv,
        )
        maps.append(m)
    return maps


_prog_cache = {}


def _prog(name, builder):
    if name not in _prog_cache:
        _prog_cache[name] = builder()
    return _prog_cache[name]


def kernel(**inputs):
    inputs = {k: np.asarray(v) for k, v in inputs.items()}
    maps = fused_inputs(inputs)
    res = run_bass_kernel_spmd(
        _prog("F", build_fused), maps,
        core_ids=list(range(NCORES))).results
    out = np.concatenate([res[c]["xout"].T for c in range(NCORES)], axis=0)
    return np.ascontiguousarray(out.reshape(B, S, D).astype(np.float32))
